# version 27
# baseline (speedup 1.0000x reference)
import numpy as np
from contextlib import ExitStack

import concourse.bass as bass
import concourse.mybir as mybir
from concourse.bass_utils import run_bass_kernel_spmd

F32 = mybir.dt.float32
BF16 = mybir.dt.bfloat16
AF = mybir.ActivationFunctionType
ALU = mybir.AluOpType

SCALE = float(2.0 ** -3.5)
EPS = 1e-5
TB = 256
NTB = 4096 // TB


class Sem:
    def __init__(self, handle):
        self.h = handle
        self.n = 0


class Sched:
    ENGINES = ("tensor", "vector", "scalar", "gpsimd", "sync")

    def __init__(self):
        self.ops = {e: [] for e in self.ENGINES}

    def emit(self, eng, fn, waits=(), incs=()):
        best = {}
        for (s, v) in waits:
            if v:
                best[s.h] = max(best.get(s.h, 0), v)
        w = list(best.items())
        ii = []
        tok = None
        for (s, k) in incs:
            s.n += k
            ii.append((s.h, k))
            tok = s.n
        self.ops[eng].append((w, fn, ii))
        return tok

    def run(self, nc):
        with nc.Block() as blk:
            for name in self.ENGINES:
                ops = self.ops[name]
                if not ops:
                    continue

                def mk(ops=ops):
                    def f(eng):
                        for waits, fn, incs in ops:
                            for (sem, v) in waits:
                                eng.wait_ge(sem, v)
                            ins = fn(eng)
                            for (sem, k) in incs:
                                ins = ins.then_inc(sem, k)
                    return f
                getattr(blk, name)(mk())


def build():
    nc = bass.Bass("TRN2", target_bir_lowering=False)

    xt_tb = nc.dram_tensor("xt_tb", [NTB, 128, 32 * TB], F32, kind="ExternalInput")
    xo_b = nc.dram_tensor("xo_b", [128, 32 * 1024], F32, kind="ExternalInput")
    xo_res = nc.dram_tensor("xo_res", [32, 128, 1024], F32, kind="ExternalInput")
    wk_h = nc.dram_tensor("wk", [128, 8 * 32 * 128], F32, kind="ExternalInput")
    wv_h = nc.dram_tensor("wv", [128, 32 * 1024], F32, kind="ExternalInput")
    wq_h = nc.dram_tensor("wq", [32, 128, 32 * 128], F32, kind="ExternalInput")
    wo_h = nc.dram_tensor("wo", [32, 128, 32 * 128], F32, kind="ExternalInput")
    bv_h = nc.dram_tensor("bv", [1, 1024], F32, kind="ExternalInput")
    up_h = nc.dram_tensor("up", [16, 128, 2048], F32, kind="ExternalInput")
    dn_h = nc.dram_tensor("dn", [16, 128, 2048], F32, kind="ExternalInput")
    out_h = nc.dram_tensor("out", [32, 128, 1024], F32, kind="ExternalOutput")
    ksc = nc.dram_tensor("ksc", [8, 128, 4096], BF16, kind="Internal")
    vsc = nc.dram_tensor("vsc", [32, 128, 8, 128], BF16, kind="Internal")
    qsc = nc.dram_tensor("qsc", [32, 128, 1024], BF16, kind="Internal")
    ysc = nc.dram_tensor("ysc", [32, 128, 1024], F32, kind="Internal")

    ctx = ExitStack()
    ones16 = ctx.enter_context(nc.sbuf_tensor("ones16", [128, 1], BF16))
    ones32 = ctx.enter_context(nc.sbuf_tensor("ones32", [1, 128], F32))

    def sem(name):
        return Sem(ctx.enter_context(nc.semaphore(name)))

    def sem2(name):
        return [sem(name + "0"), sem(name + "1")]

    mm = sem("mm")
    ev = sem("ev")
    ex = sem("ex")
    gx = sem("gx")
    s_wkc = [sem(f"swkc{i}") for i in range(8)]
    s_xoh = [sem("sxoh0"), sem("sxoh1")]
    pv = sem("pv")
    bar = sem("bar")
    s_wk, s_wv, s_bv, s_xob = sem("swk"), sem("swv"), sem("sbv"), sem("sxob")
    s_xb = sem2("sxb")
    s_kst = sem2("skst")
    s_vst = sem2("svst")
    s_wq = sem2("swq")
    s_qst = sem2("sqst")
    s_kb = sem2("skb")
    s_vb = sem2("svb")
    s_qb = sem2("sqb")
    s_wo = sem2("swo")
    s_xr = sem2("sxr")
    s_yst = sem2("syst")
    s_yr = sem2("syr")
    s_up = sem2("sup")
    s_dn = sem2("sdn")
    s_ost = sem2("sost")

    def barrier(S):
        for e in Sched.ENGINES:
            S.emit(e, lambda E: E.nop(), incs=[(bar, 1)])
        tgt = bar.n
        for e in Sched.ENGINES:
            S.emit(e, lambda E: E.nop(), waits=[(bar, tgt)])

    toks = {}

    with (
        nc.sbuf_tensor("bvrow", [1, 1024], F32) as bvrow,
        nc.sbuf_tensor("bvb", [128, 1024], F32) as bvb,
        nc.sbuf_tensor("wkbuf", [128, 8 * 32 * 128], BF16) as wkbuf,
        nc.sbuf_tensor("wvbuf", [128, 32 * 1024], BF16) as wvbuf,
        nc.sbuf_tensor("xb0", [128, 32 * TB], BF16) as xb0,
        nc.sbuf_tensor("xb1", [128, 32 * TB], BF16) as xb1,
        nc.sbuf_tensor("kstg0", [128, TB], BF16) as kstg0,
        nc.sbuf_tensor("kstg1", [128, TB], BF16) as kstg1,
        nc.sbuf_tensor("vstg0", [128, 1024], BF16) as vstg0,
        nc.sbuf_tensor("vstg1", [128, 1024], BF16) as vstg1,
        nc.psum_tensor("kpsA", [128, 512], F32) as kpsA,
        nc.psum_tensor("kpsB", [128, 512], F32) as kpsB,
        nc.psum_tensor("vpsA", [128, 1024], F32) as vpsA,
        nc.psum_tensor("vpsB", [128, 1024], F32) as vpsB,
        nc.psum_tensor("bvps", [128, 1024], F32) as bvps,
    ):
        xb = [xb0, xb1]
        kstg = [kstg0, kstg1]
        vstg = [vstg0, vstg1]
        kps = [kpsA, kpsB]
        vps = [vpsA, vpsB]
        S = Sched()

        t_bv = S.emit("sync", lambda E: E.dma_start(bvrow[:, :], bv_h.ap()), incs=[(s_bv, 16)])
        t_ones = S.emit("vector", lambda E: E.memset(ones16[:, :], 1.0), incs=[(ev, 1)])
        t_ones2 = S.emit("vector", lambda E: E.memset(ones32[:, :], 1.0), incs=[(ev, 1)])

        def fn_bvbc(E):
            E.matmul(bvps[:, 0:512], ones32[:, :], bvrow[0:1, 0:512],
                     start=True, stop=True, skip_group_check=True)
            return E.matmul(bvps[:, 512:1024], ones32[:, :], bvrow[0:1, 512:1024],
                            start=True, stop=True, skip_group_check=True)
        t_bcmm = S.emit("tensor", fn_bvbc, waits=[(s_bv, t_bv), (ev, t_ones2)], incs=[(mm, 1)])
        t_bvb = S.emit("vector", lambda E: E.tensor_copy(bvb[:, :], bvps[:, :]),
                       waits=[(mm, t_bcmm)], incs=[(ev, 1)])

        xb_tok = [None] * NTB
        wk_tok = [None] * 8
        wk_tok[0] = S.emit("gpsimd",
                           lambda E: E.dma_start(wkbuf[:, 0:4096], wk_h.ap()[:, 0:4096]),
                           incs=[(s_wkc[0], 16)])
        for _tb in range(2):
            xb_tok[_tb] = S.emit("gpsimd",
                                 lambda E, d=xb[_tb], _tb=_tb: E.dma_start(d[:, :], xt_tb.ap()[_tb]),
                                 incs=[(s_xb[_tb], 16)])
        for _c in range(1, 8):
            wk_tok[_c] = S.emit("gpsimd",
                                lambda E, _c=_c: E.dma_start(
                                    wkbuf[:, _c * 4096:(_c + 1) * 4096],
                                    wk_h.ap()[:, _c * 4096:(_c + 1) * 4096]),
                                incs=[(s_wkc[_c], 16)])
        t_wv = S.emit("gpsimd", lambda E: E.dma_start(wvbuf[:, :], wv_h.ap()), incs=[(s_wv, 16)])

        xfree = [None] * NTB
        kfree = {0: None, 1: None}
        vfree = {0: None, 1: None}
        kstg_free = {0: None, 1: None}
        vstg_free = {0: None, 1: None}
        kgrp = 0
        vgrp = 0
        for tb in range(NTB):
            p = tb % 2
            if tb >= 2:
                xb_tok[tb] = S.emit(
                    "gpsimd",
                    lambda E, d=xb[p], tb=tb: E.dma_start(d[:, :], xt_tb.ap()[tb]),
                    waits=[(mm, xfree[tb - 2])],
                    incs=[(s_xb[p], 16)])
            for c in range(8):
                kp = kgrp % 2
                kgrp += 1

                def fnK(E, kp=kp, c=c, p=p):
                    ins = None
                    for t in range(32):
                        ins = E.matmul(kps[kp][:, 0:TB],
                                       wkbuf[:, (c * 32 + t) * 128:(c * 32 + t + 1) * 128],
                                       xb[p][:, t * TB:(t + 1) * TB],
                                       start=(t == 0), stop=(t == 31))
                    return ins
                tkm = S.emit("tensor", fnK,
                             waits=[(s_xb[p], xb_tok[tb]), (s_wkc[c], wk_tok[c]), (ev, kfree[kp])],
                             incs=[(mm, 1)])
                tke = S.emit("vector",
                             lambda E, kp=kp: E.tensor_copy(kstg[kp][:, :], kps[kp][:, 0:TB]),
                             waits=[(mm, tkm), (s_kst[kp], kstg_free[kp])], incs=[(ev, 1)])
                kfree[kp] = tke
                kstg_free[kp] = S.emit(
                    "scalar",
                    lambda E, kp=kp, c=c, tb=tb: E.dma_start(
                        ksc.ap()[c][:, tb * TB:(tb + 1) * TB], kstg[kp][:, :]),
                    waits=[(ev, tke)], incs=[(s_kst[kp], 16)])
            for tc in range(2):
                vp = vgrp % 2
                vgrp += 1

                def fnV(E, vp=vp, tc=tc, p=p):
                    ins = None
                    for t in range(32):
                        for nb in range(2):
                            ins = E.matmul(vps[vp][:, nb * 512:(nb + 1) * 512],
                                           xb[p][:, t * TB + tc * 128: t * TB + tc * 128 + 128],
                                           wvbuf[:, t * 1024 + nb * 512: t * 1024 + (nb + 1) * 512],
                                           start=(t == 0), stop=(t == 31))
                    return ins
                tvm = S.emit("tensor", fnV,
                             waits=[(s_xb[p], xb_tok[tb]), (s_wv, t_wv), (ev, vfree[vp])],
                             incs=[(mm, 1)])
                if tc == 1:
                    xfree[tb] = tvm
                tve = S.emit("vector",
                             lambda E, vp=vp: E.tensor_tensor(vstg[vp][:, :], vps[vp][:, :],
                                                              bvb[:, :], op=ALU.add),
                             waits=[(mm, tvm), (s_vst[vp], vstg_free[vp]), (ev, t_bvb)],
                             incs=[(ev, 1)])
                vfree[vp] = tve
                vstg_free[vp] = S.emit(
                    "scalar",
                    lambda E, vp=vp, kt=tb * 2 + tc: E.dma_start(
                        vsc.ap()[kt], vstg[vp][:, :].rearrange("p (h f) -> p h f", f=128)),
                    waits=[(ev, tve)], incs=[(s_vst[vp], 16)])
        S.emit("scalar", lambda E: E.nop(),
               waits=[(s_kst[0], s_kst[0].n), (s_kst[1], s_kst[1].n),
                      (s_vst[0], s_vst[0].n), (s_vst[1], s_vst[1].n)])
        barrier(S)
        S.run(nc)

    with (
        nc.sbuf_tensor("xob", [128, 32 * 1024], BF16) as xob,
        nc.sbuf_tensor("wqb0", [128, 32 * 128], BF16) as wqb0,
        nc.sbuf_tensor("wqb1", [128, 32 * 128], BF16) as wqb1,
        nc.sbuf_tensor("qstg0", [128, 1024], BF16) as qstg0,
        nc.sbuf_tensor("qstg1", [128, 1024], BF16) as qstg1,
        nc.psum_tensor("qpsA", [128, 1024], F32) as qpsA,
        nc.psum_tensor("qpsB", [128, 1024], F32) as qpsB,
    ):
        wqb = [wqb0, wqb1]
        qstg = [qstg0, qstg1]
        qps = [qpsA, qpsB]
        S = Sched()
        def load_xoh(i):
            return S.emit("gpsimd",
                          lambda E, i=i: E.dma_start(xob[:, i * 16384:(i + 1) * 16384],
                                                     xo_b.ap()[:, i * 16384:(i + 1) * 16384]),
                          incs=[(s_xoh[i], 16)])
        t_xoh = [None, None]
        t_xoh[0] = load_xoh(0)
        wq_tok = [None] * 32
        wqfree = {0: None, 1: None}
        qps_free = {0: None, 1: None}
        qstg_free = {0: None, 1: None}
        for c in range(32):
            p = c % 2
            wq_tok[c] = S.emit("gpsimd",
                               lambda E, p=p, c=c: E.dma_start(wqb[p][:, :], wq_h.ap()[c]),
                               waits=[(mm, wqfree[p])], incs=[(s_wq[p], 16)])
            if c == 0:
                t_xoh[1] = load_xoh(1)

            def fnQh(E, p=p, h=0):
                ins = None
                for t in range(h * 16, h * 16 + 16):
                    for nb in range(2):
                        ins = E.matmul(qps[p][:, nb * 512:(nb + 1) * 512],
                                       wqb[p][:, t * 128:(t + 1) * 128],
                                       xob[:, t * 1024 + nb * 512: t * 1024 + (nb + 1) * 512],
                                       start=(t == 0), stop=(t == 31))
                return ins
            S.emit("tensor", lambda E, p=p: fnQh(E, p, 0),
                   waits=[(s_xoh[0], t_xoh[0]), (s_wq[p], wq_tok[c]), (ev, qps_free[p])],
                   incs=[(mm, 1)])
            tqm = S.emit("tensor", lambda E, p=p: fnQh(E, p, 1),
                         waits=[(s_xoh[1], t_xoh[1])],
                         incs=[(mm, 1)])
            wqfree[p] = tqm
            tqe = S.emit("vector",
                         lambda E, p=p: E.tensor_copy(qstg[p][:, :], qps[p][:, :]),
                         waits=[(mm, tqm), (s_qst[p], qstg_free[p])], incs=[(ev, 1)])
            qps_free[p] = tqe
            qstg_free[p] = S.emit(
                "scalar",
                lambda E, p=p, c=c: E.dma_start(qsc.ap()[c], qstg[p][:, :]),
                waits=[(ev, tqe)], incs=[(s_qst[p], 16)])
        S.emit("scalar", lambda E: E.nop(),
               waits=[(s_qst[0], s_qst[0].n), (s_qst[1], s_qst[1].n)])
        barrier(S)
        S.run(nc)

    ctx2 = ExitStack()
    mb = ctx2.enter_context(nc.sbuf_tensor("mb", [128, 1024], F32))
    rb = ctx2.enter_context(nc.sbuf_tensor("rb", [128, 1024], F32))

    with (
        nc.sbuf_tensor("ctxb", [128, 32 * 1024], BF16) as ctxb,
        nc.sbuf_tensor("wob0", [128, 32 * 128], BF16) as wob0,
        nc.sbuf_tensor("wob1", [128, 32 * 128], BF16) as wob1,
    ):
        wob = [wob0, wob1]
        with ExitStack() as es2:
            invs_row = es2.enter_context(nc.sbuf_tensor("invs_row", [1, 512], F32))
            ln_row = es2.enter_context(nc.sbuf_tensor("ln_row", [1, 512], F32))
            invs_bc = es2.enter_context(nc.sbuf_tensor("invs_bc", [128, 512], F32))
            kb = [es2.enter_context(nc.sbuf_tensor(f"kb{i}", [128, 4096], BF16)) for i in range(2)]
            vb = [es2.enter_context(nc.sbuf_tensor(f"vb{i}", [128, 4096], BF16)) for i in range(2)]
            qb = [es2.enter_context(nc.sbuf_tensor(f"qb{i}", [128, 1024], BF16)) for i in range(2)]
            pb = [es2.enter_context(nc.sbuf_tensor(f"pb{i}", [128, 1024], BF16)) for i in range(4)]
            padd = [es2.enter_context(nc.sbuf_tensor(f"padd{i}", [128, 512], BF16)) for i in range(4)]
            sc_ps = [es2.enter_context(nc.psum_tensor(f"sc{i}", [128, 1024], F32)) for i in range(2)]
            cx_ps = [es2.enter_context(nc.psum_tensor(f"cx{i}", [128, 512], F32)) for i in range(2)]
            sm_ps = [es2.enter_context(nc.psum_tensor(f"sm{i}", [128, 512], F32)) for i in range(2)]
            S = Sched()
            kv_free = {0: None, 1: None}
            q_free = {0: None, 1: None}
            pb_war = {0: None, 1: None, 2: None, 3: None}
            pb_war_v = {0: None, 1: None, 2: None, 3: None}
            pa_war = {0: None, 1: None, 2: None, 3: None}
            sc_exp = {0: None, 1: None}
            cs_free = {0: None, 1: None}
            state = {"last_bc": None, "tail": None, "last_ie": None}

            def emit_tail():
                (u_tok, upar, qh, sco) = state["tail"]
                state["tail"] = None
                t_ln = S.emit("scalar",
                              lambda E, upar=upar: E.activation(ln_row[0:1, :],
                                                                sm_ps[upar][0:1, :], AF.Ln),
                              waits=[(mm, u_tok), (ex, state["last_ie"])], incs=[(ex, 1)])
                t_ie = S.emit("scalar",
                              lambda E: E.activation(invs_row[0:1, :], ln_row[0:1, :],
                                                     AF.Exp, scale=-1.0),
                              waits=[(ex, t_ln), (mm, state["last_bc"])], incs=[(ex, 1)])
                state["last_ie"] = t_ie
                t_bc = S.emit("tensor",
                              lambda E, upar=upar: E.matmul(sm_ps[upar][:, :], ones32[:, :],
                                                            invs_row[0:1, :], start=True,
                                                            stop=True, skip_group_check=True),
                              waits=[(ex, t_ie)], incs=[(mm, 1)])
                state["last_bc"] = t_bc
                t_cp = S.emit("vector",
                              lambda E, upar=upar: E.tensor_copy(invs_bc[:, :], sm_ps[upar][:, :]),
                              waits=[(mm, t_bc)], incs=[(ev, 1)])
                t_cx = S.emit("vector",
                              lambda E, upar=upar, qh=qh, sco=sco: E.tensor_tensor(
                                  ctxb[:, qh * 1024 + sco: qh * 1024 + sco + 512],
                                  cx_ps[upar][:, :], invs_bc[:, :], op=ALU.mult),
                              waits=[(ev, t_cp)], incs=[(ev, 1)])
                cs_free[upar] = t_cx

            for h in range(8):
                hp = h % 2
                t_k = S.emit("sync", lambda E, hp=hp, h=h: E.dma_start(kb[hp][:, :], ksc.ap()[h]),
                             waits=[(mm, kv_free[hp])], incs=[(s_kb[hp], 16)])
                t_v = S.emit("sync",
                             lambda E, hp=hp, h=h: E.dma_start(
                                 vb[hp][:, :].rearrange("p (t f) -> p t f", f=128),
                                 vsc.ap()[:, :, h, :].transpose([1, 0, 2])),
                             waits=[(mm, kv_free[hp])], incs=[(s_vb[hp], 16)])
                for g in range(4):
                    qh = h * 4 + g
                    qp = qh % 2
                    t_q = S.emit("sync", lambda E, qp=qp, qh=qh: E.dma_start(qb[qp][:, :], qsc.ap()[qh]),
                                 waits=[(mm, q_free[qp])], incs=[(s_qb[qp], 16)])
                    for qc in range(2):
                        upar = (qh * 2 + qc) % 2
                        sco = qc * 512

                        def emit_scores(grp, first):
                            cp = grp % 2

                            def fnS(E, grp=grp, cp=cp, hp=hp, qp=qp, sco=sco):
                                E.matmul(sc_ps[cp][:, 0:512],
                                         kb[hp][:, (2 * grp) * 128:(2 * grp + 1) * 128],
                                         qb[qp][:, sco:sco + 512],
                                         start=True, stop=True, skip_group_check=True)
                                return E.matmul(sc_ps[cp][:, 512:1024],
                                                kb[hp][:, (2 * grp + 1) * 128:(2 * grp + 2) * 128],
                                                qb[qp][:, sco:sco + 512],
                                                start=True, stop=True, skip_group_check=True)
                            waits = [(ex, sc_exp[cp])]
                            if first:
                                waits += [(s_kb[hp], t_k), (s_qb[qp], t_q)]
                            return S.emit("tensor", fnS, waits=waits, incs=[(mm, 1)])

                        def emit_exp(grp, tok_s):
                            sp = grp % 4
                            cp = grp % 2
                            t = S.emit("scalar",
                                       lambda E, sp=sp, cp=cp: E.activation(pb[sp][:, :], sc_ps[cp][:, :],
                                                                            AF.Exp, scale=SCALE),
                                       waits=[(mm, tok_s), (mm, pb_war[sp])]
                                       + ([pb_war_v[sp]] if pb_war_v[sp] else []),
                                       incs=[(ex, 1)])
                            sc_exp[cp] = t
                            return t

                        def emit_padd(grp, tok_e):
                            sp = grp % 4
                            eng, msem = (("vector", pv) if grp % 2 == 0 else ("gpsimd", gx))
                            t = S.emit(eng,
                                       lambda E, sp=sp: E.tensor_tensor(
                                           padd[sp][:, :], pb[sp][:, 0:512],
                                           pb[sp][:, 512:1024], op=ALU.add),
                                       waits=[(ex, tok_e), (mm, pa_war[sp])], incs=[(msem, 1)])
                            pb_war_v[sp] = (msem, t)
                            return (msem, t)

                        def emit_ctx(grp, tok_e):
                            sp = grp % 4

                            def fnC(E, grp=grp, sp=sp, hp=hp, upar=upar):
                                E.matmul(cx_ps[upar][:, :],
                                         vb[hp][:, (2 * grp) * 128:(2 * grp + 1) * 128],
                                         pb[sp][:, 0:512],
                                         start=(grp == 0), stop=False, skip_group_check=True)
                                return E.matmul(cx_ps[upar][:, :],
                                                vb[hp][:, (2 * grp + 1) * 128:(2 * grp + 2) * 128],
                                                pb[sp][:, 512:1024],
                                                start=False, stop=(grp == 15), skip_group_check=True)
                            waits = [(ex, tok_e)]
                            if grp == 0:
                                waits += [(s_vb[hp], t_v), (ev, cs_free[upar])]
                            t = S.emit("tensor", fnC, waits=waits, incs=[(mm, 1)])
                            pb_war[sp] = t
                            return t

                        def emit_sum(grp, tok_p):
                            sp = grp % 4
                            t = S.emit("tensor",
                                       lambda E, sp=sp, grp=grp, upar=upar: E.matmul(
                                           sm_ps[upar][0:1, :], ones16[:, :], padd[sp][:, :],
                                           start=(grp == 0), stop=(grp == 15),
                                           skip_group_check=True),
                                       waits=[tok_p], incs=[(mm, 1)])
                            pa_war[sp] = t
                            return t

                        s_toks = {}
                        s_toks[0] = emit_scores(0, True)
                        s_toks[1] = emit_scores(1, False)
                        p_toks = {}
                        u_tok = None
                        for grp in range(16):
                            tok_e = emit_exp(grp, s_toks[grp])
                            p_toks[grp] = emit_padd(grp, tok_e)
                            sp = grp % 4
                            cp = grp % 2

                            def fnCS(E, grp=grp, sp=sp, cp=cp, hp=hp, qp=qp, sco=sco, upar=upar):
                                E.matmul(cx_ps[upar][:, :],
                                         vb[hp][:, (2 * grp) * 128:(2 * grp + 1) * 128],
                                         pb[sp][:, 0:512],
                                         start=(grp == 0), stop=False, skip_group_check=True)
                                ins = E.matmul(cx_ps[upar][:, :],
                                               vb[hp][:, (2 * grp + 1) * 128:(2 * grp + 2) * 128],
                                               pb[sp][:, 512:1024],
                                               start=False, stop=(grp == 15), skip_group_check=True)
                                if grp + 2 < 16:
                                    g2 = grp + 2
                                    E.matmul(sc_ps[cp][:, 0:512],
                                             kb[hp][:, (2 * g2) * 128:(2 * g2 + 1) * 128],
                                             qb[qp][:, sco:sco + 512],
                                             start=True, stop=True, skip_group_check=True)
                                    ins = E.matmul(sc_ps[cp][:, 512:1024],
                                                   kb[hp][:, (2 * g2 + 1) * 128:(2 * g2 + 2) * 128],
                                                   qb[qp][:, sco:sco + 512],
                                                   start=True, stop=True, skip_group_check=True)
                                return ins
                            waits = [(ex, tok_e)]
                            if grp == 0:
                                waits += [(s_vb[hp], t_v), (ev, cs_free[upar])]
                            t_cs = S.emit("tensor", fnCS, waits=waits, incs=[(mm, 1)])
                            pb_war[sp] = t_cs
                            if grp + 2 < 16:
                                s_toks[grp + 2] = t_cs
                            if grp == 0 and state["tail"] is not None:
                                emit_tail()
                            if grp >= 2:
                                emit_sum(grp - 2, p_toks[grp - 2])
                        emit_sum(14, p_toks[14])
                        u_tok = emit_sum(15, p_toks[15])
                        if qc == 1:
                            q_free[qp] = s_toks[15]
                            if g == 3:
                                kv_free[hp] = u_tok
                        state["tail"] = (u_tok, upar, qh, sco)
            emit_tail()
            for _c in range(2):
                toks[f"wo{_c}"] = S.emit(
                    "gpsimd",
                    lambda E, _c=_c: E.dma_start(wob[_c][:, :], wo_h.ap()[_c]),
                    incs=[(s_wo[_c], 16)])
            barrier(S)
            S.run(nc)

        with (
            nc.psum_tensor("stS", [128, 1024], F32) as stS,
            nc.psum_tensor("stQ", [128, 1024], F32) as stQ,
        ):
            with ExitStack() as es3:
                m_row = es3.enter_context(nc.sbuf_tensor("m_row", [1, 1024], F32))
                msq_row = es3.enter_context(nc.sbuf_tensor("msq_row", [1, 1024], F32))
                m2_row = es3.enter_context(nc.sbuf_tensor("m2_row", [1, 1024], F32))
                var_row = es3.enter_context(nc.sbuf_tensor("var_row", [1, 1024], F32))
                lnv_row = es3.enter_context(nc.sbuf_tensor("lnv_row", [1, 1024], F32))
                rstd_row = es3.enter_context(nc.sbuf_tensor("rstd_row", [1, 1024], F32))
                xr0 = es3.enter_context(nc.sbuf_tensor("xr0", [128, 1024], F32))
                xr1 = es3.enter_context(nc.sbuf_tensor("xr1", [128, 1024], F32))
                yb0 = es3.enter_context(nc.sbuf_tensor("yb0", [128, 1024], F32))
                yb1 = es3.enter_context(nc.sbuf_tensor("yb1", [128, 1024], F32))
                y16a = es3.enter_context(nc.sbuf_tensor("y16a", [128, 1024], BF16))
                y16b = es3.enter_context(nc.sbuf_tensor("y16b", [128, 1024], BF16))
                ysqa = es3.enter_context(nc.sbuf_tensor("ysqa", [128, 1024], BF16))
                ysqb = es3.enter_context(nc.sbuf_tensor("ysqb", [128, 1024], BF16))
                atA = es3.enter_context(nc.psum_tensor("atA", [128, 1024], F32))
                atB = es3.enter_context(nc.psum_tensor("atB", [128, 1024], F32))
                xr = [xr0, xr1]
                yb = [yb0, yb1]
                y16 = [y16a, y16b]
                ysq = [ysqa, ysqb]
                at_ps = [atA, atB]
                S = Sched()
                wofree = {0: None, 1: None}
                xrfree = {0: None, 1: None}
                at_free = {0: None, 1: None}
                yb_free = {0: None, 1: None}
                sq_war = {0: None, 1: None}
                stg_war = {0: None, 1: None}
                t_stats = None
                pend = {"s": None}

                def emit_stats():
                    (p, c, t_sq, t_16) = pend["s"]
                    pend["s"] = None

                    def fnStats(E, p=p, c=c):
                        E.matmul(stS[0:1, 0:512], ones16[:, :], y16[p][:, 0:512],
                                 start=(c == 0), stop=(c == 31), skip_group_check=True)
                        E.matmul(stS[0:1, 512:1024], ones16[:, :], y16[p][:, 512:1024],
                                 start=(c == 0), stop=(c == 31), skip_group_check=True)
                        E.matmul(stQ[0:1, 0:512], ones16[:, :], ysq[p][:, 0:512],
                                 start=(c == 0), stop=(c == 31), skip_group_check=True)
                        return E.matmul(stQ[0:1, 512:1024], ones16[:, :], ysq[p][:, 512:1024],
                                        start=(c == 0), stop=(c == 31), skip_group_check=True)
                    t = S.emit("tensor", fnStats,
                               waits=[(ex, t_sq), (ev, t_16)], incs=[(mm, 1)])
                    stg_war[p] = t
                    return t

                for c in range(32):
                    p = c % 2
                    if c < 2:
                        t_wo = toks[f"wo{c}"]
                    else:
                        t_wo = S.emit("gpsimd", lambda E, p=p, c=c: E.dma_start(wob[p][:, :], wo_h.ap()[c]),
                                      waits=[(mm, wofree[p])], incs=[(s_wo[p], 16)])
                    t_xr = S.emit("sync", lambda E, p=p, c=c: E.dma_start(xr[p][:, :], xo_res.ap()[c]),
                                  waits=[(ev, xrfree[p])], incs=[(s_xr[p], 16)])

                    def fnWo(E, p=p):
                        ins = None
                        for t in range(32):
                            for nb in range(2):
                                ins = E.matmul(at_ps[p][:, nb * 512:(nb + 1) * 512],
                                               wob[p][:, t * 128:(t + 1) * 128],
                                               ctxb[:, t * 1024 + nb * 512: t * 1024 + (nb + 1) * 512],
                                               start=(t == 0), stop=(t == 31), skip_group_check=True)
                        return ins
                    t_at = S.emit("tensor", fnWo,
                                  waits=[(s_wo[p], t_wo), (ev, at_free[p])], incs=[(mm, 1)])
                    wofree[p] = t_at
                    if pend["s"] is not None:
                        t_stats = emit_stats()
                    t_y = S.emit("vector",
                                 lambda E, p=p: E.tensor_tensor(yb[p][:, :], at_ps[p][:, :],
                                                                xr[p][:, :], op=ALU.add),
                                 waits=[(mm, t_at), (s_xr[p], t_xr), (s_yst[p], yb_free[p]),
                                        (ex, sq_war[p])],
                                 incs=[(ev, 1)])
                    at_free[p] = t_y
                    xrfree[p] = t_y
                    yb_free[p] = S.emit("scalar",
                                        lambda E, p=p, c=c: E.dma_start(ysc.ap()[c], yb[p][:, :]),
                                        waits=[(ev, t_y)], incs=[(s_yst[p], 16)])
                    t_sq = S.emit("scalar",
                                  lambda E, p=p: E.activation(ysq[p][:, :], yb[p][:, :], AF.Square),
                                  waits=[(ev, t_y), (mm, stg_war[p])], incs=[(ex, 1)])
                    sq_war[p] = t_sq
                    t_16 = S.emit("vector",
                                  lambda E, p=p: E.tensor_copy(y16[p][:, :], yb[p][:, :]),
                                  waits=[(mm, stg_war[p]), (ev, t_y)], incs=[(ev, 1)])
                    pend["s"] = (p, c, t_sq, t_16)
                t_stats = emit_stats()
                toks["stats1"] = t_stats
                t1 = S.emit("vector", lambda E: E.tensor_scalar_mul(m_row[:, :], stS[0:1, :], 1.0 / 4096.0),
                            waits=[(mm, toks["stats1"])], incs=[(ev, 1)])
                t2 = S.emit("vector", lambda E: E.tensor_scalar_mul(msq_row[:, :], stQ[0:1, :], 1.0 / 4096.0),
                            waits=[(ev, t1)], incs=[(ev, 1)])
                t3 = S.emit("vector", lambda E: E.tensor_tensor(m2_row[:, :], m_row[:, :], m_row[:, :], op=ALU.mult),
                            waits=[(ev, t2)], incs=[(ev, 1)])
                t4 = S.emit("vector", lambda E: E.tensor_tensor(var_row[:, :], msq_row[:, :], m2_row[:, :], op=ALU.subtract),
                            waits=[(ev, t3)], incs=[(ev, 1)])
                t4b = S.emit("vector", lambda E: E.tensor_scalar_add(var_row[:, :], var_row[:, :], EPS),
                             waits=[(ev, t4)], incs=[(ev, 1)])
                t5 = S.emit("scalar", lambda E: E.activation(lnv_row[:, :], var_row[:, :], AF.Ln),
                            waits=[(ev, t4b)], incs=[(ex, 1)])
                t6 = S.emit("scalar", lambda E: E.activation(rstd_row[:, :], lnv_row[:, :], AF.Exp, scale=-0.5),
                            waits=[(ex, t5)], incs=[(ex, 1)])

                def fn_bc(E):
                    E.matmul(stS[:, 0:512], ones32[:, :], m_row[0:1, 0:512],
                             start=True, stop=True, skip_group_check=True)
                    E.matmul(stS[:, 512:1024], ones32[:, :], m_row[0:1, 512:1024],
                             start=True, stop=True, skip_group_check=True)
                    E.matmul(stQ[:, 0:512], ones32[:, :], rstd_row[0:1, 0:512],
                             start=True, stop=True, skip_group_check=True)
                    return E.matmul(stQ[:, 512:1024], ones32[:, :], rstd_row[0:1, 512:1024],
                                    start=True, stop=True, skip_group_check=True)
                t7 = S.emit("tensor", fn_bc, waits=[(ex, t6), (ev, t2)], incs=[(mm, 1)])
                t8 = S.emit("vector", lambda E: E.tensor_copy(mb[:, :], stS[:, :]),
                            waits=[(mm, t7)], incs=[(ev, 1)])
                t9 = S.emit("vector", lambda E: E.tensor_copy(rb[:, :], stQ[:, :]),
                            waits=[(ev, t8)], incs=[(ev, 1)])
                toks["mbrb"] = t9
                S.emit("scalar", lambda E: E.nop(),
                       waits=[(s_yst[0], s_yst[0].n), (s_yst[1], s_yst[1].n)])
                barrier(S)
                S.run(nc)

    with nc.sbuf_tensor("x2", [128, 32 * 1024], F32) as x2:
        with (
            nc.sbuf_tensor("up0", [128, 2048], BF16) as up0,
            nc.sbuf_tensor("up1", [128, 2048], BF16) as up1,
            nc.sbuf_tensor("dn0", [128, 2048], BF16) as dn0,
            nc.sbuf_tensor("dn1", [128, 2048], BF16) as dn1,
            nc.sbuf_tensor("xp0", [128, 2048], BF16) as xp0,
            nc.sbuf_tensor("g16a", [128, 8 * 1024], BF16) as g16a,
            nc.sbuf_tensor("g16b", [128, 8 * 1024], BF16) as g16b,
            nc.sbuf_tensor("yr0", [128, 1024], F32) as yr0,
            nc.sbuf_tensor("yr1", [128, 1024], F32) as yr1,
            nc.sbuf_tensor("z16s", [128, 1024], BF16) as z16s,
            nc.sbuf_tensor("zsqs", [128, 1024], BF16) as zsqs,
            nc.sbuf_tensor("rowb", [128, 1024], F32) as rowb,
            nc.psum_tensor("upA", [128, 1024], F32) as upA,
            nc.psum_tensor("upB", [128, 1024], F32) as upB,
            nc.psum_tensor("dnA", [128, 1024], F32) as dnA,
            nc.psum_tensor("stP", [128, 1024], F32) as stP,
        ):
            upb = [up0, up1]
            dnb = [dn0, dn1]
            xpb = [xp0, xp0]
            g16 = [g16a, g16b]
            yr = [yr0, yr1]
            up_ps = [upA, upB]
            S = Sched()
            x2_tok = [None] * 32
            yr_free = {0: None, 1: None}
            norm_state = {"next": 0}

            def emit_norm(c):
                p = c % 2
                eng, msem = (("gpsimd", gx) if c % 3 == 2 else ("vector", ev))
                t_l = S.emit("sync", lambda E, p=p, c=c: E.dma_start(yr[p][:, :], ysc.ap()[c]),
                             waits=[yr_free[p]] if yr_free[p] else [], incs=[(s_yr[p], 16)])
                t_s = S.emit(eng,
                             lambda E, p=p: E.tensor_tensor(yr[p][:, :], yr[p][:, :], mb[:, :],
                                                            op=ALU.subtract),
                             waits=[(s_yr[p], t_l), (ev, toks["mbrb"]),
                                    (x2_tok[c - 2] if c >= 2 else None) or (ev, None)],
                             incs=[(msem, 1)])
                x2_tok[c] = (msem, S.emit(eng,
                                          lambda E, p=p, c=c: E.tensor_tensor(
                                              x2[:, c * 1024:(c + 1) * 1024], yr[p][:, :], rb[:, :],
                                              op=ALU.mult),
                                          waits=[(msem, t_s)], incs=[(msem, 1)]))
                yr_free[p] = x2_tok[c]

            def ensure_norm(upto):
                while norm_state["next"] <= min(upto, 31):
                    emit_norm(norm_state["next"])
                    norm_state["next"] += 1

            res_tok = [None] * 32
            upfree = {0: None, 1: None}
            dnfree = {0: None, 1: None}
            xp_war = {0: None}
            upps_war = {0: None, 1: None}
            g16_war = {0: None, 1: None}
            gelu_last = {0: None, 1: None}
            dn_tok = {0: None, 1: None}
            stat_war = None
            stat_last = None
            pend_stats = []

            def emit_stats():
                (c, t_sqz, t_z) = pend_stats.pop(0)

                def fnStats(E, c=c):
                    E.matmul(stP[0:1, 0:512], ones16[:, :], z16s[:, 0:512],
                             start=(c == 0), stop=False, skip_group_check=True)
                    E.matmul(stP[0:1, 512:1024], ones16[:, :], z16s[:, 512:1024],
                             start=(c == 0), stop=False, skip_group_check=True)
                    E.matmul(stP[32:33, 0:512], ones16[:, :], zsqs[:, 0:512],
                             start=False, stop=(c == 31), skip_group_check=True)
                    return E.matmul(stP[32:33, 512:1024], ones16[:, :], zsqs[:, 512:1024],
                                    start=False, stop=(c == 31), skip_group_check=True)
                return S.emit("tensor", fnStats,
                              waits=[(ex, t_sqz), (ev, t_z)], incs=[(mm, 1)])

            def queue_stats(c):
                t_sqz = S.emit("scalar",
                               lambda E, c=c: E.activation(zsqs[:, :],
                                                           x2[:, c * 1024:(c + 1) * 1024],
                                                           AF.Square),
                               waits=[(ev, res_tok[c]), (mm, stat_war)], incs=[(ex, 1)])
                t_z = S.emit("vector",
                             lambda E, c=c: E.tensor_copy(z16s[:, :],
                                                          x2[:, c * 1024:(c + 1) * 1024]),
                             waits=[(ev, res_tok[c]), (mm, stat_war)], incs=[(ev, 1)])
                pend_stats.append((c, t_sqz, t_z))

            def emit_dn(h, dc):
                hp = h % 2
                cidx = 2 * h + dc

                def fnDn(E, hp=hp, dc=dc):
                    ins = None
                    for t in range(8):
                        for nb in range(2):
                            ins = E.matmul(dnA[:, nb * 512:(nb + 1) * 512],
                                           dnb[hp][:, t * 256 + dc * 128: t * 256 + dc * 128 + 128],
                                           g16[hp][:, t * 1024 + nb * 512: t * 1024 + nb * 512 + 512],
                                           start=(t == 0), stop=(t == 7), skip_group_check=True)
                    return ins
                waits = [(ex, gelu_last[hp])]
                if cidx >= 1:
                    waits.append((ev, res_tok[cidx - 1]))
                if dc == 0:
                    waits.append((s_dn[hp], dn_tok[hp]))
                t_dmm = S.emit("tensor", fnDn, waits=waits, incs=[(mm, 1)])
                res_tok[cidx] = S.emit(
                    "vector",
                    lambda E, cidx=cidx: E.tensor_tensor(
                        x2[:, cidx * 1024:(cidx + 1) * 1024], dnA[:, :],
                        x2[:, cidx * 1024:(cidx + 1) * 1024], op=ALU.add),
                    waits=[(mm, t_dmm)], incs=[(ev, 1)])
                queue_stats(cidx)
                if dc == 1:
                    dnfree[hp] = t_dmm
                    g16_war[hp] = t_dmm
                return t_dmm

            for h in range(16):
                hp = h % 2
                ensure_norm(2 * h + 3)
                t_up = S.emit("gpsimd", lambda E, hp=hp, h=h: E.dma_start(upb[hp][:, :], up_h.ap()[h]),
                              waits=[(mm, upfree[hp])], incs=[(s_up[hp], 16)])
                dn_tok[hp] = S.emit("gpsimd", lambda E, hp=hp, h=h: E.dma_start(dnb[hp][:, :], dn_h.ap()[h]),
                                    waits=[(mm, dnfree[hp])], incs=[(s_dn[hp], 16)])
                t_xp = S.emit("vector",
                              lambda E, h=h: E.tensor_copy(xpb[0][:, :],
                                                           x2[:, 2 * h * 1024:(2 * h + 2) * 1024]),
                              waits=[x2_tok[2 * h], x2_tok[2 * h + 1], (mm, xp_war[0])],
                              incs=[(ev, 1)])
                up_last = None
                for mc in range(8):
                    mp = mc % 2

                    def fnUp(E, hp=hp, mc=mc, mp=mp):
                        ins = None
                        for dt in range(2):
                            for nb in range(2):
                                ins = E.matmul(up_ps[mp][:, nb * 512:(nb + 1) * 512],
                                               upb[hp][:, dt * 1024 + mc * 128: dt * 1024 + (mc + 1) * 128],
                                               xpb[0][:, dt * 1024 + nb * 512: dt * 1024 + nb * 512 + 512],
                                               start=(dt == 0), stop=(dt == 1), skip_group_check=True)
                        return ins
                    waits = [(ev, t_xp), (ex, upps_war[mp])]
                    if mc == 0:
                        waits.append((s_up[hp], t_up))
                    t_umm = S.emit("tensor", fnUp, waits=waits, incs=[(mm, 1)])
                    up_last = t_umm
                    gwaits = [(mm, t_umm), (mm, g16_war[hp])]
                    gelu_last[hp] = S.emit("scalar",
                                           lambda E, hp=hp, mp=mp, mc=mc: E.activation(
                                               g16[hp][:, mc * 1024:(mc + 1) * 1024],
                                               up_ps[mp][:, :], AF.Gelu),
                                           waits=gwaits, incs=[(ex, 1)])
                    upps_war[mp] = gelu_last[hp]
                    if h > 0 and mc == 2:
                        emit_dn(h - 1, 0)
                    if h > 0 and mc == 5:
                        emit_dn(h - 1, 1)
                    if pend_stats and mc in (3, 6):
                        stat_war = emit_stats()
                        stat_last = stat_war
                upfree[hp] = up_last
                xp_war[0] = up_last
            emit_dn(15, 0)
            stat_war = emit_stats()
            stat_last = stat_war
            emit_dn(15, 1)
            while pend_stats:
                stat_war = emit_stats()
                stat_last = stat_war

            m_r = rowb[0:1, :]
            v_r = yr1[0:1, :]
            l_r = dnA[0:1, 0:1024]
            r_r = yr0[0:1, :]
            t1 = S.emit("vector", lambda E: E.tensor_scalar_mul(m_r, stP[0:1, :], 1.0 / 4096.0),
                        waits=[(mm, stat_last)], incs=[(ev, 1)])
            t2 = S.emit("vector", lambda E: E.tensor_scalar_mul(v_r, stP[32:33, :], 1.0 / 4096.0),
                        waits=[(ev, t1)], incs=[(ev, 1)])
            t3 = S.emit("vector", lambda E: E.tensor_tensor(l_r, m_r, m_r, op=ALU.mult),
                        waits=[(ev, t2)], incs=[(ev, 1)])
            t4 = S.emit("vector", lambda E: E.tensor_tensor(v_r, v_r, l_r, op=ALU.subtract),
                        waits=[(ev, t3)], incs=[(ev, 1)])
            t4b = S.emit("vector", lambda E: E.tensor_scalar_add(v_r, v_r, EPS),
                         waits=[(ev, t4)], incs=[(ev, 1)])
            t5 = S.emit("scalar", lambda E: E.activation(l_r, v_r, AF.Ln),
                        waits=[(ev, t4b)], incs=[(ex, 1)])
            t6 = S.emit("scalar", lambda E: E.activation(r_r, l_r, AF.Exp, scale=-0.5),
                        waits=[(ex, t5)], incs=[(ex, 1)])

            def fn_bc(E):
                E.matmul(stP[:, 0:512], ones32[:, :], m_r[:, 0:512],
                         start=True, stop=True, skip_group_check=True)
                E.matmul(stP[:, 512:1024], ones32[:, :], m_r[:, 512:1024],
                         start=True, stop=True, skip_group_check=True)
                E.matmul(dnA[:, 0:512], ones32[:, :], r_r[:, 0:512],
                         start=True, stop=True, skip_group_check=True)
                return E.matmul(dnA[:, 512:1024], ones32[:, :], r_r[:, 512:1024],
                                start=True, stop=True, skip_group_check=True)
            t7 = S.emit("tensor", fn_bc, waits=[(ex, t6), (ev, t2), (ev, res_tok[31])],
                        incs=[(mm, 1)])
            t8 = S.emit("vector", lambda E: E.tensor_copy(mb[:, :], stP[:, :]),
                        waits=[(mm, t7)], incs=[(ev, 1)])
            t9 = S.emit("vector", lambda E: E.tensor_copy(rb[:, :], dnA[:, :]),
                        waits=[(ev, t8)], incs=[(ev, 1)])
            mbrb2 = t9

            ob_free = {0: None, 1: None}
            tm_free = {0: None, 1: None}
            for c in range(32):
                p = c % 2
                eng, msem = (("gpsimd", gx) if c % 3 == 2 else ("vector", ev))
                t_s = S.emit(eng,
                             lambda E, c=c: E.tensor_tensor(x2[:, c * 1024:(c + 1) * 1024],
                                                            x2[:, c * 1024:(c + 1) * 1024],
                                                            mb[:, :], op=ALU.subtract),
                             waits=[(ev, mbrb2)], incs=[(msem, 1)])
                t_m = S.emit(eng,
                             lambda E, p=p, c=c: E.tensor_tensor(yr[p][:, :],
                                                                 x2[:, c * 1024:(c + 1) * 1024],
                                                                 rb[:, :], op=ALU.mult),
                             waits=[(msem, t_s), (s_ost[p], ob_free[p]),
                                    tm_free[p] or (ev, None)],
                             incs=[(msem, 1)])
                ob_free[p] = S.emit("scalar",
                                    lambda E, p=p, c=c: E.dma_start(out_h.ap()[c], yr[p][:, :]),
                                    waits=[(msem, t_m)], incs=[(s_ost[p], 16)])
                tm_free[p] = (msem, t_m)
            S.emit("scalar", lambda E: E.nop(),
                   waits=[(s_ost[0], s_ost[0].n), (s_ost[1], s_ost[1].n)])
            barrier(S)
            S.run(nc)

    ctx2.close()
    ctx.close()
    return nc


_NC_CACHE = [None]


def _host_stage(inputs):
    X = np.asarray(inputs["L0_attn_hidden_states"], dtype=np.float32)
    Wq = np.asarray(inputs["L0_attn_Wq"], dtype=np.float32)
    Wk = np.asarray(inputs["L0_attn_Wk"], dtype=np.float32)
    Wv = np.asarray(inputs["L0_attn_Wv"], dtype=np.float32)
    bv = np.asarray(inputs["L0_attn_bv"], dtype=np.float32)
    Wo = np.asarray(inputs["L0_attn_Wo"], dtype=np.float32)
    up = np.asarray(inputs["L0_ffn_up"], dtype=np.float32)
    down = np.asarray(inputs["L0_ffn_down"], dtype=np.float32)
    perm = np.asarray(inputs["L0_ffn_perm"]).astype(np.int64)

    XT = [np.ascontiguousarray(X[b].T[perm, :]) for b in range(2)]
    xt_tb = [np.ascontiguousarray(
        XT[b].reshape(32, 128, NTB, TB).transpose(2, 1, 0, 3)).reshape(NTB, 128, 32 * TB)
        for b in range(2)]
    wk_host = np.ascontiguousarray(
        Wk.T[perm].reshape(32, 128, 8, 128).transpose(1, 2, 0, 3)).reshape(128, 8 * 32 * 128)
    wv_host = np.ascontiguousarray(
        Wv.T[perm].reshape(32, 128, 1024).transpose(1, 0, 2)).reshape(128, 32 * 1024)
    wq_host = np.ascontiguousarray(
        Wq.T[perm].reshape(32, 128, 32, 128).transpose(2, 1, 0, 3)).reshape(32, 128, 32 * 128)
    wo_host = np.ascontiguousarray(
        Wo[perm].T.reshape(32, 128, 32, 128).transpose(2, 1, 0, 3)).reshape(32, 128, 32 * 128)
    bv_host = np.ascontiguousarray(bv.reshape(1, 1024))
    up_host = np.ascontiguousarray(
        up.reshape(16, 2, 128, 1024).transpose(0, 2, 1, 3)).reshape(16, 128, 2048)
    dn_host = np.ascontiguousarray(
        down.reshape(16, 8, 128, 256).transpose(0, 2, 1, 3)).reshape(16, 128, 2048)

    in_maps = []
    for core in range(8):
        b, t0 = core // 4, (core % 4) * 1024
        own = XT[b][:, t0:t0 + 1024]
        xo_b = np.ascontiguousarray(
            own.reshape(32, 128, 1024).transpose(1, 0, 2)).reshape(128, 32 * 1024)
        xo_res = np.ascontiguousarray(own.reshape(32, 128, 1024))
        in_maps.append({
            "xt_tb": xt_tb[b], "xo_b": xo_b, "xo_res": xo_res,
            "wk": wk_host, "wv": wv_host, "wq": wq_host, "wo": wo_host,
            "bv": bv_host, "up": up_host, "dn": dn_host,
        })
    return in_maps, perm


def kernel(**inputs):
    in_maps, perm = _host_stage(inputs)
    if _NC_CACHE[0] is None:
        _NC_CACHE[0] = build()
    nc = _NC_CACHE[0]
    res = run_bass_kernel_spmd(nc, in_maps, core_ids=list(range(8)))
    out = np.empty((2, 4096, 4096), dtype=np.float32)
    for core in range(8):
        b, t0 = core // 4, (core % 4) * 1024
        o = np.asarray(res.results[core]["out"]).reshape(4096, 1024)
        out[b, t0:t0 + 1024][:, perm] = o.T
    return out


# version 28
# speedup vs baseline: 1.0282x; 1.0282x over previous
import numpy as np
from contextlib import ExitStack

import concourse.bass as bass
import concourse.mybir as mybir
from concourse.bass_utils import run_bass_kernel_spmd

F32 = mybir.dt.float32
BF16 = mybir.dt.bfloat16
AF = mybir.ActivationFunctionType
ALU = mybir.AluOpType

SCALE = float(2.0 ** -3.5)
EPS = 1e-5
TB = 256
NTB = 4096 // TB


class Sem:
    def __init__(self, handle):
        self.h = handle
        self.n = 0


class Sched:
    ENGINES = ("tensor", "vector", "scalar", "gpsimd", "sync")

    def __init__(self):
        self.ops = {e: [] for e in self.ENGINES}

    def emit(self, eng, fn, waits=(), incs=()):
        best = {}
        for (s, v) in waits:
            if v:
                best[s.h] = max(best.get(s.h, 0), v)
        w = list(best.items())
        ii = []
        tok = None
        for (s, k) in incs:
            s.n += k
            ii.append((s.h, k))
            tok = s.n
        self.ops[eng].append((w, fn, ii))
        return tok

    def run(self, nc):
        with nc.Block() as blk:
            for name in self.ENGINES:
                ops = self.ops[name]
                if not ops:
                    continue

                def mk(ops=ops):
                    def f(eng):
                        for waits, fn, incs in ops:
                            for (sem, v) in waits:
                                eng.wait_ge(sem, v)
                            ins = fn(eng)
                            for (sem, k) in incs:
                                ins = ins.then_inc(sem, k)
                    return f
                getattr(blk, name)(mk())


def build():
    nc = bass.Bass("TRN2", target_bir_lowering=False)

    xt_tb = nc.dram_tensor("xt_tb", [NTB, 128, 32 * TB], F32, kind="ExternalInput")
    xo_b = nc.dram_tensor("xo_b", [128, 32 * 1024], F32, kind="ExternalInput")
    xo_res = nc.dram_tensor("xo_res", [32, 128, 1024], F32, kind="ExternalInput")
    wk_h = nc.dram_tensor("wk", [128, 8 * 32 * 128], F32, kind="ExternalInput")
    wv_h = nc.dram_tensor("wv", [128, 32 * 1024], F32, kind="ExternalInput")
    wq_h = nc.dram_tensor("wq", [32, 128, 32 * 128], F32, kind="ExternalInput")
    wo_h = nc.dram_tensor("wo", [32, 128, 32 * 128], F32, kind="ExternalInput")
    bv_h = nc.dram_tensor("bv", [1, 1024], F32, kind="ExternalInput")
    up_h = nc.dram_tensor("up", [16, 128, 2048], F32, kind="ExternalInput")
    dn_h = nc.dram_tensor("dn", [16, 128, 2048], F32, kind="ExternalInput")
    out_h = nc.dram_tensor("out", [32, 128, 1024], F32, kind="ExternalOutput")
    ksc = nc.dram_tensor("ksc", [8, 128, 4096], BF16, kind="Internal")
    vsc = nc.dram_tensor("vsc", [32, 128, 8, 128], BF16, kind="Internal")
    qsc = nc.dram_tensor("qsc", [32, 128, 1024], BF16, kind="Internal")
    ysc = nc.dram_tensor("ysc", [32, 128, 1024], F32, kind="Internal")

    ctx = ExitStack()
    ones16 = ctx.enter_context(nc.sbuf_tensor("ones16", [128, 1], BF16))
    ones32 = ctx.enter_context(nc.sbuf_tensor("ones32", [1, 128], F32))

    def sem(name):
        return Sem(ctx.enter_context(nc.semaphore(name)))

    def sem2(name):
        return [sem(name + "0"), sem(name + "1")]

    mm = sem("mm")
    ev = sem("ev")
    ex = sem("ex")
    gx = sem("gx")
    s_wkc = [sem(f"swkc{i}") for i in range(8)]
    s_xoh = [sem("sxoh0"), sem("sxoh1")]
    pv = sem("pv")
    bar = sem("bar")
    s_wk, s_wv, s_bv, s_xob = sem("swk"), sem("swv"), sem("sbv"), sem("sxob")
    s_xb = sem2("sxb")
    s_kst = sem2("skst")
    s_vst = sem2("svst")
    s_wq = sem2("swq")
    s_qst = sem2("sqst")
    s_kb = sem2("skb")
    s_vb = sem2("svb")
    s_qb = sem2("sqb")
    s_wo = sem2("swo")
    s_xr = sem2("sxr")
    s_yst = sem2("syst")
    s_yr = sem2("syr")
    s_up = sem2("sup")
    s_dn = sem2("sdn")
    s_ost = sem2("sost")

    def barrier(S):
        for e in Sched.ENGINES:
            S.emit(e, lambda E: E.nop(), incs=[(bar, 1)])
        tgt = bar.n
        for e in Sched.ENGINES:
            S.emit(e, lambda E: E.nop(), waits=[(bar, tgt)])

    toks = {}

    with (
        nc.sbuf_tensor("bvrow", [1, 1024], F32) as bvrow,
        nc.sbuf_tensor("bvb", [128, 1024], F32) as bvb,
        nc.sbuf_tensor("wkbuf", [128, 8 * 32 * 128], BF16) as wkbuf,
        nc.sbuf_tensor("wvbuf", [128, 32 * 1024], BF16) as wvbuf,
        nc.sbuf_tensor("xb0", [128, 32 * TB], BF16) as xb0,
        nc.sbuf_tensor("xb1", [128, 32 * TB], BF16) as xb1,
        nc.sbuf_tensor("kstg0", [128, TB], BF16) as kstg0,
        nc.sbuf_tensor("kstg1", [128, TB], BF16) as kstg1,
        nc.sbuf_tensor("vstg0", [128, 1024], BF16) as vstg0,
        nc.sbuf_tensor("vstg1", [128, 1024], BF16) as vstg1,
        nc.psum_tensor("kpsA", [128, 512], F32) as kpsA,
        nc.psum_tensor("kpsB", [128, 512], F32) as kpsB,
        nc.psum_tensor("vpsA", [128, 1024], F32) as vpsA,
        nc.psum_tensor("vpsB", [128, 1024], F32) as vpsB,
        nc.psum_tensor("bvps", [128, 1024], F32) as bvps,
    ):
        xb = [xb0, xb1]
        kstg = [kstg0, kstg1]
        vstg = [vstg0, vstg1]
        kps = [kpsA, kpsB]
        vps = [vpsA, vpsB]
        S = Sched()

        t_bv = S.emit("sync", lambda E: E.dma_start(bvrow[:, :], bv_h.ap()), incs=[(s_bv, 16)])
        t_ones = S.emit("vector", lambda E: E.memset(ones16[:, :], 1.0), incs=[(ev, 1)])
        t_ones2 = S.emit("vector", lambda E: E.memset(ones32[:, :], 1.0), incs=[(ev, 1)])

        def fn_bvbc(E):
            E.matmul(bvps[:, 0:512], ones32[:, :], bvrow[0:1, 0:512],
                     start=True, stop=True, skip_group_check=True)
            return E.matmul(bvps[:, 512:1024], ones32[:, :], bvrow[0:1, 512:1024],
                            start=True, stop=True, skip_group_check=True)
        t_bcmm = S.emit("tensor", fn_bvbc, waits=[(s_bv, t_bv), (ev, t_ones2)], incs=[(mm, 1)])
        t_bvb = S.emit("vector", lambda E: E.tensor_copy(bvb[:, :], bvps[:, :]),
                       waits=[(mm, t_bcmm)], incs=[(ev, 1)])

        xb_tok = [None] * NTB
        wk_tok = [None] * 8
        wk_tok[0] = S.emit("gpsimd",
                           lambda E: E.dma_start(wkbuf[:, 0:4096], wk_h.ap()[:, 0:4096]),
                           incs=[(s_wkc[0], 16)])
        for _tb in range(2):
            xb_tok[_tb] = S.emit("gpsimd",
                                 lambda E, d=xb[_tb], _tb=_tb: E.dma_start(d[:, :], xt_tb.ap()[_tb]),
                                 incs=[(s_xb[_tb], 16)])
        for _c in range(1, 8):
            wk_tok[_c] = S.emit("gpsimd",
                                lambda E, _c=_c: E.dma_start(
                                    wkbuf[:, _c * 4096:(_c + 1) * 4096],
                                    wk_h.ap()[:, _c * 4096:(_c + 1) * 4096]),
                                incs=[(s_wkc[_c], 16)])
        t_wv = S.emit("gpsimd", lambda E: E.dma_start(wvbuf[:, :], wv_h.ap()), incs=[(s_wv, 16)])

        xfree = [None] * NTB
        kfree = {0: None, 1: None}
        vfree = {0: None, 1: None}
        kstg_free = {0: None, 1: None}
        vstg_free = {0: None, 1: None}
        kgrp = 0
        vgrp = 0
        for tb in range(NTB):
            p = tb % 2
            if tb >= 2:
                xb_tok[tb] = S.emit(
                    "gpsimd",
                    lambda E, d=xb[p], tb=tb: E.dma_start(d[:, :], xt_tb.ap()[tb]),
                    waits=[(mm, xfree[tb - 2])],
                    incs=[(s_xb[p], 16)])
            for c in range(8):
                kp = kgrp % 2
                kgrp += 1

                def fnK(E, kp=kp, c=c, p=p):
                    ins = None
                    for t in range(32):
                        ins = E.matmul(kps[kp][:, 0:TB],
                                       wkbuf[:, (c * 32 + t) * 128:(c * 32 + t + 1) * 128],
                                       xb[p][:, t * TB:(t + 1) * TB],
                                       start=(t == 0), stop=(t == 31))
                    return ins
                tkm = S.emit("tensor", fnK,
                             waits=[(s_xb[p], xb_tok[tb]), (s_wkc[c], wk_tok[c]), (ev, kfree[kp])],
                             incs=[(mm, 1)])
                tke = S.emit("vector",
                             lambda E, kp=kp: E.tensor_copy(kstg[kp][:, :], kps[kp][:, 0:TB]),
                             waits=[(mm, tkm), (s_kst[kp], kstg_free[kp])], incs=[(ev, 1)])
                kfree[kp] = tke
                kstg_free[kp] = S.emit(
                    "scalar",
                    lambda E, kp=kp, c=c, tb=tb: E.dma_start(
                        ksc.ap()[c][:, tb * TB:(tb + 1) * TB], kstg[kp][:, :]),
                    waits=[(ev, tke)], incs=[(s_kst[kp], 16)])
            for tc in range(2):
                vp = vgrp % 2
                vgrp += 1

                def fnV(E, vp=vp, tc=tc, p=p):
                    ins = None
                    for t in range(32):
                        for nb in range(2):
                            ins = E.matmul(vps[vp][:, nb * 512:(nb + 1) * 512],
                                           xb[p][:, t * TB + tc * 128: t * TB + tc * 128 + 128],
                                           wvbuf[:, t * 1024 + nb * 512: t * 1024 + (nb + 1) * 512],
                                           start=(t == 0), stop=(t == 31))
                    return ins
                tvm = S.emit("tensor", fnV,
                             waits=[(s_xb[p], xb_tok[tb]), (s_wv, t_wv), (ev, vfree[vp])],
                             incs=[(mm, 1)])
                if tc == 1:
                    xfree[tb] = tvm
                tve = S.emit("vector",
                             lambda E, vp=vp: E.tensor_tensor(vstg[vp][:, :], vps[vp][:, :],
                                                              bvb[:, :], op=ALU.add),
                             waits=[(mm, tvm), (s_vst[vp], vstg_free[vp]), (ev, t_bvb)],
                             incs=[(ev, 1)])
                vfree[vp] = tve
                vstg_free[vp] = S.emit(
                    "scalar",
                    lambda E, vp=vp, kt=tb * 2 + tc: E.dma_start(
                        vsc.ap()[kt], vstg[vp][:, :].rearrange("p (h f) -> p h f", f=128)),
                    waits=[(ev, tve)], incs=[(s_vst[vp], 16)])
        S.emit("scalar", lambda E: E.nop(),
               waits=[(s_kst[0], s_kst[0].n), (s_kst[1], s_kst[1].n),
                      (s_vst[0], s_vst[0].n), (s_vst[1], s_vst[1].n)])
        barrier(S)
        S.run(nc)

    with (
        nc.sbuf_tensor("xob", [128, 32 * 1024], BF16) as xob,
        nc.sbuf_tensor("wqb0", [128, 32 * 128], BF16) as wqb0,
        nc.sbuf_tensor("wqb1", [128, 32 * 128], BF16) as wqb1,
        nc.sbuf_tensor("qstg0", [128, 1024], BF16) as qstg0,
        nc.sbuf_tensor("qstg1", [128, 1024], BF16) as qstg1,
        nc.psum_tensor("qpsA", [128, 1024], F32) as qpsA,
        nc.psum_tensor("qpsB", [128, 1024], F32) as qpsB,
    ):
        wqb = [wqb0, wqb1]
        qstg = [qstg0, qstg1]
        qps = [qpsA, qpsB]
        S = Sched()
        def load_xoh(i):
            return S.emit("gpsimd",
                          lambda E, i=i: E.dma_start(xob[:, i * 16384:(i + 1) * 16384],
                                                     xo_b.ap()[:, i * 16384:(i + 1) * 16384]),
                          incs=[(s_xoh[i], 16)])
        t_xoh = [None, None]
        t_xoh[0] = load_xoh(0)
        wq_tok = [None] * 32
        wqfree = {0: None, 1: None}
        qps_free = {0: None, 1: None}
        qstg_free = {0: None, 1: None}
        for c in range(32):
            p = c % 2
            wq_tok[c] = S.emit("gpsimd",
                               lambda E, p=p, c=c: E.dma_start(wqb[p][:, :], wq_h.ap()[c]),
                               waits=[(mm, wqfree[p])], incs=[(s_wq[p], 16)])
            if c == 0:
                t_xoh[1] = load_xoh(1)

            def fnQh(E, p=p, h=0):
                ins = None
                for t in range(h * 16, h * 16 + 16):
                    for nb in range(2):
                        ins = E.matmul(qps[p][:, nb * 512:(nb + 1) * 512],
                                       wqb[p][:, t * 128:(t + 1) * 128],
                                       xob[:, t * 1024 + nb * 512: t * 1024 + (nb + 1) * 512],
                                       start=(t == 0), stop=(t == 31))
                return ins
            S.emit("tensor", lambda E, p=p: fnQh(E, p, 0),
                   waits=[(s_xoh[0], t_xoh[0]), (s_wq[p], wq_tok[c]), (ev, qps_free[p])],
                   incs=[(mm, 1)])
            tqm = S.emit("tensor", lambda E, p=p: fnQh(E, p, 1),
                         waits=[(s_xoh[1], t_xoh[1])],
                         incs=[(mm, 1)])
            wqfree[p] = tqm
            tqe = S.emit("vector",
                         lambda E, p=p: E.tensor_copy(qstg[p][:, :], qps[p][:, :]),
                         waits=[(mm, tqm), (s_qst[p], qstg_free[p])], incs=[(ev, 1)])
            qps_free[p] = tqe
            qstg_free[p] = S.emit(
                "scalar",
                lambda E, p=p, c=c: E.dma_start(qsc.ap()[c], qstg[p][:, :]),
                waits=[(ev, tqe)], incs=[(s_qst[p], 16)])
        S.emit("scalar", lambda E: E.nop(),
               waits=[(s_qst[0], s_qst[0].n), (s_qst[1], s_qst[1].n)])
        barrier(S)
        S.run(nc)

    ctx2 = ExitStack()
    mb = ctx2.enter_context(nc.sbuf_tensor("mb", [128, 1024], F32))
    rb = ctx2.enter_context(nc.sbuf_tensor("rb", [128, 1024], F32))

    with (
        nc.sbuf_tensor("ctxb", [128, 32 * 1024], BF16) as ctxb,
        nc.sbuf_tensor("wob0", [128, 32 * 128], BF16) as wob0,
        nc.sbuf_tensor("wob1", [128, 32 * 128], BF16) as wob1,
    ):
        wob = [wob0, wob1]
        with ExitStack() as es2:
            invs_row = es2.enter_context(nc.sbuf_tensor("invs_row", [1, 512], F32))
            ln_row = es2.enter_context(nc.sbuf_tensor("ln_row", [1, 512], F32))
            invs_bc = es2.enter_context(nc.sbuf_tensor("invs_bc", [128, 512], F32))
            kb = [es2.enter_context(nc.sbuf_tensor(f"kb{i}", [128, 4096], BF16)) for i in range(2)]
            vb = [es2.enter_context(nc.sbuf_tensor(f"vb{i}", [128, 4096], BF16)) for i in range(2)]
            qb = [es2.enter_context(nc.sbuf_tensor(f"qb{i}", [128, 1024], BF16)) for i in range(2)]
            pb = [es2.enter_context(nc.sbuf_tensor(f"pb{i}", [128, 1024], BF16)) for i in range(4)]
            padd = [es2.enter_context(nc.sbuf_tensor(f"padd{i}", [128, 512], BF16)) for i in range(4)]
            sc_ps = [es2.enter_context(nc.psum_tensor(f"sc{i}", [128, 1024], F32)) for i in range(2)]
            cx_ps = [es2.enter_context(nc.psum_tensor(f"cx{i}", [128, 512], F32)) for i in range(2)]
            sm_ps = [es2.enter_context(nc.psum_tensor(f"sm{i}", [128, 512], F32)) for i in range(2)]
            S = Sched()
            kv_free = {0: None, 1: None}
            q_free = {0: None, 1: None}
            pb_war = {0: None, 1: None, 2: None, 3: None}
            pb_war_v = {0: None, 1: None, 2: None, 3: None}
            pa_war = {0: None, 1: None, 2: None, 3: None}
            sc_exp = {0: None, 1: None}
            cs_free = {0: None, 1: None}
            state = {"last_bc": None, "tail": None, "last_ie": None}

            def emit_tail():
                (u_tok, upar, qh, sco) = state["tail"]
                state["tail"] = None
                t_ln = S.emit("scalar",
                              lambda E, upar=upar: E.activation(ln_row[0:1, :],
                                                                sm_ps[upar][0:1, :], AF.Ln),
                              waits=[(mm, u_tok), (ex, state["last_ie"])], incs=[(ex, 1)])
                t_ie = S.emit("scalar",
                              lambda E: E.activation(invs_row[0:1, :], ln_row[0:1, :],
                                                     AF.Exp, scale=-1.0),
                              waits=[(ex, t_ln), (mm, state["last_bc"])], incs=[(ex, 1)])
                state["last_ie"] = t_ie
                t_bc = S.emit("tensor",
                              lambda E, upar=upar: E.matmul(sm_ps[upar][:, :], ones32[:, :],
                                                            invs_row[0:1, :], start=True,
                                                            stop=True, skip_group_check=True),
                              waits=[(ex, t_ie)], incs=[(mm, 1)])
                state["last_bc"] = t_bc
                t_cp = S.emit("vector",
                              lambda E, upar=upar: E.tensor_copy(invs_bc[:, :], sm_ps[upar][:, :]),
                              waits=[(mm, t_bc)], incs=[(ev, 1)])
                t_cx = S.emit("vector",
                              lambda E, upar=upar, qh=qh, sco=sco: E.tensor_tensor(
                                  ctxb[:, qh * 1024 + sco: qh * 1024 + sco + 512],
                                  cx_ps[upar][:, :], invs_bc[:, :], op=ALU.mult),
                              waits=[(ev, t_cp)], incs=[(ev, 1)])
                cs_free[upar] = t_cx

            for h in range(8):
                hp = h % 2
                t_k = S.emit("sync", lambda E, hp=hp, h=h: E.dma_start(kb[hp][:, :], ksc.ap()[h]),
                             waits=[(mm, kv_free[hp])], incs=[(s_kb[hp], 16)])
                t_v = S.emit("sync",
                             lambda E, hp=hp, h=h: E.dma_start(
                                 vb[hp][:, :].rearrange("p (t f) -> p t f", f=128),
                                 vsc.ap()[:, :, h, :].transpose([1, 0, 2])),
                             waits=[(mm, kv_free[hp])], incs=[(s_vb[hp], 16)])
                for g in range(4):
                    qh = h * 4 + g
                    qp = qh % 2
                    t_q = S.emit("sync", lambda E, qp=qp, qh=qh: E.dma_start(qb[qp][:, :], qsc.ap()[qh]),
                                 waits=[(mm, q_free[qp])], incs=[(s_qb[qp], 16)])
                    for qc in range(2):
                        upar = (qh * 2 + qc) % 2
                        sco = qc * 512

                        def emit_scores(grp, first):
                            cp = grp % 2

                            def fnS(E, grp=grp, cp=cp, hp=hp, qp=qp, sco=sco):
                                E.matmul(sc_ps[cp][:, 0:512],
                                         kb[hp][:, (2 * grp) * 128:(2 * grp + 1) * 128],
                                         qb[qp][:, sco:sco + 512],
                                         start=True, stop=True, skip_group_check=True)
                                return E.matmul(sc_ps[cp][:, 512:1024],
                                                kb[hp][:, (2 * grp + 1) * 128:(2 * grp + 2) * 128],
                                                qb[qp][:, sco:sco + 512],
                                                start=True, stop=True, skip_group_check=True)
                            waits = [(ex, sc_exp[cp])]
                            if first:
                                waits += [(s_kb[hp], t_k), (s_qb[qp], t_q)]
                            return S.emit("tensor", fnS, waits=waits, incs=[(mm, 1)])

                        def emit_exp(grp, tok_s):
                            sp = grp % 4
                            cp = grp % 2
                            t = S.emit("scalar",
                                       lambda E, sp=sp, cp=cp: E.activation(pb[sp][:, :], sc_ps[cp][:, :],
                                                                            AF.Exp, scale=SCALE),
                                       waits=[(mm, tok_s), (mm, pb_war[sp])]
                                       + ([pb_war_v[sp]] if pb_war_v[sp] else []),
                                       incs=[(ex, 1)])
                            sc_exp[cp] = t
                            return t

                        def emit_padd(grp, tok_e):
                            sp = grp % 4
                            eng, msem = (("vector", pv) if grp % 2 == 0 else ("gpsimd", gx))
                            t = S.emit(eng,
                                       lambda E, sp=sp: E.tensor_tensor(
                                           padd[sp][:, :], pb[sp][:, 0:512],
                                           pb[sp][:, 512:1024], op=ALU.add),
                                       waits=[(ex, tok_e), (mm, pa_war[sp])], incs=[(msem, 1)])
                            pb_war_v[sp] = (msem, t)
                            return (msem, t)

                        def emit_ctx(grp, tok_e):
                            sp = grp % 4

                            def fnC(E, grp=grp, sp=sp, hp=hp, upar=upar):
                                E.matmul(cx_ps[upar][:, :],
                                         vb[hp][:, (2 * grp) * 128:(2 * grp + 1) * 128],
                                         pb[sp][:, 0:512],
                                         start=(grp == 0), stop=False, skip_group_check=True)
                                return E.matmul(cx_ps[upar][:, :],
                                                vb[hp][:, (2 * grp + 1) * 128:(2 * grp + 2) * 128],
                                                pb[sp][:, 512:1024],
                                                start=False, stop=(grp == 15), skip_group_check=True)
                            waits = [(ex, tok_e)]
                            if grp == 0:
                                waits += [(s_vb[hp], t_v), (ev, cs_free[upar])]
                            t = S.emit("tensor", fnC, waits=waits, incs=[(mm, 1)])
                            pb_war[sp] = t
                            return t

                        def emit_sum(grp, tok_p):
                            sp = grp % 4
                            t = S.emit("tensor",
                                       lambda E, sp=sp, grp=grp, upar=upar: E.matmul(
                                           sm_ps[upar][0:1, :], ones16[:, :], padd[sp][:, :],
                                           start=(grp == 0), stop=(grp == 15),
                                           skip_group_check=True),
                                       waits=[tok_p], incs=[(mm, 1)])
                            pa_war[sp] = t
                            return t

                        s_toks = {}
                        s_toks[0] = emit_scores(0, True)
                        s_toks[1] = emit_scores(1, False)
                        p_toks = {}
                        u_tok = None
                        for grp in range(16):
                            tok_e = emit_exp(grp, s_toks[grp])
                            p_toks[grp] = emit_padd(grp, tok_e)
                            sp = grp % 4
                            cp = grp % 2

                            def fnCS(E, grp=grp, sp=sp, cp=cp, hp=hp, qp=qp, sco=sco, upar=upar):
                                E.matmul(cx_ps[upar][:, :],
                                         vb[hp][:, (2 * grp) * 128:(2 * grp + 1) * 128],
                                         pb[sp][:, 0:512],
                                         start=(grp == 0), stop=False, skip_group_check=True)
                                ins = E.matmul(cx_ps[upar][:, :],
                                               vb[hp][:, (2 * grp + 1) * 128:(2 * grp + 2) * 128],
                                               pb[sp][:, 512:1024],
                                               start=False, stop=(grp == 15), skip_group_check=True)
                                if grp + 2 < 16:
                                    g2 = grp + 2
                                    E.matmul(sc_ps[cp][:, 0:512],
                                             kb[hp][:, (2 * g2) * 128:(2 * g2 + 1) * 128],
                                             qb[qp][:, sco:sco + 512],
                                             start=True, stop=True, skip_group_check=True)
                                    ins = E.matmul(sc_ps[cp][:, 512:1024],
                                                   kb[hp][:, (2 * g2 + 1) * 128:(2 * g2 + 2) * 128],
                                                   qb[qp][:, sco:sco + 512],
                                                   start=True, stop=True, skip_group_check=True)
                                return ins
                            waits = [(ex, tok_e)]
                            if grp == 0:
                                waits += [(s_vb[hp], t_v), (ev, cs_free[upar])]
                            t_cs = S.emit("tensor", fnCS, waits=waits, incs=[(mm, 1)])
                            pb_war[sp] = t_cs
                            if grp + 2 < 16:
                                s_toks[grp + 2] = t_cs
                            if grp == 0 and state["tail"] is not None:
                                emit_tail()
                            if grp >= 2:
                                emit_sum(grp - 2, p_toks[grp - 2])
                        emit_sum(14, p_toks[14])
                        u_tok = emit_sum(15, p_toks[15])
                        if qc == 1:
                            q_free[qp] = s_toks[15]
                            if g == 3:
                                kv_free[hp] = u_tok
                        state["tail"] = (u_tok, upar, qh, sco)
            emit_tail()
            for _c in range(2):
                toks[f"wo{_c}"] = S.emit(
                    "gpsimd",
                    lambda E, _c=_c: E.dma_start(wob[_c][:, :], wo_h.ap()[_c]),
                    incs=[(s_wo[_c], 16)])
            barrier(S)
            S.run(nc)

        with (
            nc.psum_tensor("stS", [128, 1024], F32) as stS,
            nc.psum_tensor("stQ", [128, 1024], F32) as stQ,
        ):
            with ExitStack() as es3:
                m_row = es3.enter_context(nc.sbuf_tensor("m_row", [1, 1024], F32))
                msq_row = es3.enter_context(nc.sbuf_tensor("msq_row", [1, 1024], F32))
                m2_row = es3.enter_context(nc.sbuf_tensor("m2_row", [1, 1024], F32))
                var_row = es3.enter_context(nc.sbuf_tensor("var_row", [1, 1024], F32))
                lnv_row = es3.enter_context(nc.sbuf_tensor("lnv_row", [1, 1024], F32))
                rstd_row = es3.enter_context(nc.sbuf_tensor("rstd_row", [1, 1024], F32))
                xr0 = es3.enter_context(nc.sbuf_tensor("xr0", [128, 1024], F32))
                xr1 = es3.enter_context(nc.sbuf_tensor("xr1", [128, 1024], F32))
                yb0 = es3.enter_context(nc.sbuf_tensor("yb0", [128, 1024], F32))
                yb1 = es3.enter_context(nc.sbuf_tensor("yb1", [128, 1024], F32))
                y16a = es3.enter_context(nc.sbuf_tensor("y16a", [128, 1024], BF16))
                y16b = es3.enter_context(nc.sbuf_tensor("y16b", [128, 1024], BF16))
                ysqa = es3.enter_context(nc.sbuf_tensor("ysqa", [128, 1024], BF16))
                ysqb = es3.enter_context(nc.sbuf_tensor("ysqb", [128, 1024], BF16))
                atA = es3.enter_context(nc.psum_tensor("atA", [128, 1024], F32))
                atB = es3.enter_context(nc.psum_tensor("atB", [128, 1024], F32))
                xr = [xr0, xr1]
                yb = [yb0, yb1]
                y16 = [y16a, y16b]
                ysq = [ysqa, ysqb]
                at_ps = [atA, atB]
                S = Sched()
                wofree = {0: None, 1: None}
                xrfree = {0: None, 1: None}
                at_free = {0: None, 1: None}
                yb_free = {0: None, 1: None}
                sq_war = {0: None, 1: None}
                stg_war = {0: None, 1: None}
                t_stats = None
                pend = {"s": None}

                def emit_stats():
                    (p, c, t_sq, t_16) = pend["s"]
                    pend["s"] = None

                    def fnStats(E, p=p, c=c):
                        E.matmul(stS[0:1, 0:512], ones16[:, :], y16[p][:, 0:512],
                                 start=(c == 0), stop=(c == 31), skip_group_check=True)
                        E.matmul(stS[0:1, 512:1024], ones16[:, :], y16[p][:, 512:1024],
                                 start=(c == 0), stop=(c == 31), skip_group_check=True)
                        E.matmul(stQ[0:1, 0:512], ones16[:, :], ysq[p][:, 0:512],
                                 start=(c == 0), stop=(c == 31), skip_group_check=True)
                        return E.matmul(stQ[0:1, 512:1024], ones16[:, :], ysq[p][:, 512:1024],
                                        start=(c == 0), stop=(c == 31), skip_group_check=True)
                    t = S.emit("tensor", fnStats,
                               waits=[(ex, t_sq), (ev, t_16)], incs=[(mm, 1)])
                    stg_war[p] = t
                    return t

                for c in range(32):
                    p = c % 2
                    if c < 2:
                        t_wo = toks[f"wo{c}"]
                    else:
                        t_wo = S.emit("gpsimd", lambda E, p=p, c=c: E.dma_start(wob[p][:, :], wo_h.ap()[c]),
                                      waits=[(mm, wofree[p])], incs=[(s_wo[p], 16)])
                    t_xr = S.emit("sync", lambda E, p=p, c=c: E.dma_start(xr[p][:, :], xo_res.ap()[c]),
                                  waits=[(ev, xrfree[p])], incs=[(s_xr[p], 16)])

                    def fnWo(E, p=p):
                        ins = None
                        for t in range(32):
                            for nb in range(2):
                                ins = E.matmul(at_ps[p][:, nb * 512:(nb + 1) * 512],
                                               wob[p][:, t * 128:(t + 1) * 128],
                                               ctxb[:, t * 1024 + nb * 512: t * 1024 + (nb + 1) * 512],
                                               start=(t == 0), stop=(t == 31), skip_group_check=True)
                        return ins
                    t_at = S.emit("tensor", fnWo,
                                  waits=[(s_wo[p], t_wo), (ev, at_free[p])], incs=[(mm, 1)])
                    wofree[p] = t_at
                    if pend["s"] is not None:
                        t_stats = emit_stats()
                    t_y = S.emit("vector",
                                 lambda E, p=p: E.tensor_tensor(yb[p][:, :], at_ps[p][:, :],
                                                                xr[p][:, :], op=ALU.add),
                                 waits=[(mm, t_at), (s_xr[p], t_xr), (s_yst[p], yb_free[p]),
                                        (ex, sq_war[p])],
                                 incs=[(ev, 1)])
                    at_free[p] = t_y
                    xrfree[p] = t_y
                    yb_free[p] = S.emit("scalar",
                                        lambda E, p=p, c=c: E.dma_start(ysc.ap()[c], yb[p][:, :]),
                                        waits=[(ev, t_y)], incs=[(s_yst[p], 16)])
                    t_sq = S.emit("scalar",
                                  lambda E, p=p: E.activation(ysq[p][:, :], yb[p][:, :], AF.Square),
                                  waits=[(ev, t_y), (mm, stg_war[p])], incs=[(ex, 1)])
                    sq_war[p] = t_sq
                    t_16 = S.emit("vector",
                                  lambda E, p=p: E.tensor_copy(y16[p][:, :], yb[p][:, :]),
                                  waits=[(mm, stg_war[p]), (ev, t_y)], incs=[(ev, 1)])
                    pend["s"] = (p, c, t_sq, t_16)
                t_stats = emit_stats()
                toks["stats1"] = t_stats
                t1 = S.emit("vector", lambda E: E.tensor_scalar_mul(m_row[:, :], stS[0:1, :], 1.0 / 4096.0),
                            waits=[(mm, toks["stats1"])], incs=[(ev, 1)])
                t2 = S.emit("vector", lambda E: E.tensor_scalar_mul(msq_row[:, :], stQ[0:1, :], 1.0 / 4096.0),
                            waits=[(ev, t1)], incs=[(ev, 1)])
                t3 = S.emit("vector", lambda E: E.tensor_tensor(m2_row[:, :], m_row[:, :], m_row[:, :], op=ALU.mult),
                            waits=[(ev, t2)], incs=[(ev, 1)])
                t4 = S.emit("vector", lambda E: E.tensor_tensor(var_row[:, :], msq_row[:, :], m2_row[:, :], op=ALU.subtract),
                            waits=[(ev, t3)], incs=[(ev, 1)])
                t4b = S.emit("vector", lambda E: E.tensor_scalar_add(var_row[:, :], var_row[:, :], EPS),
                             waits=[(ev, t4)], incs=[(ev, 1)])
                t5 = S.emit("scalar", lambda E: E.activation(lnv_row[:, :], var_row[:, :], AF.Ln),
                            waits=[(ev, t4b)], incs=[(ex, 1)])
                t6 = S.emit("scalar", lambda E: E.activation(rstd_row[:, :], lnv_row[:, :], AF.Exp, scale=-0.5),
                            waits=[(ex, t5)], incs=[(ex, 1)])

                def fn_bc(E):
                    E.matmul(stS[:, 0:512], ones32[:, :], m_row[0:1, 0:512],
                             start=True, stop=True, skip_group_check=True)
                    E.matmul(stS[:, 512:1024], ones32[:, :], m_row[0:1, 512:1024],
                             start=True, stop=True, skip_group_check=True)
                    E.matmul(stQ[:, 0:512], ones32[:, :], rstd_row[0:1, 0:512],
                             start=True, stop=True, skip_group_check=True)
                    return E.matmul(stQ[:, 512:1024], ones32[:, :], rstd_row[0:1, 512:1024],
                                    start=True, stop=True, skip_group_check=True)
                t7 = S.emit("tensor", fn_bc, waits=[(ex, t6), (ev, t2)], incs=[(mm, 1)])
                t8 = S.emit("vector", lambda E: E.tensor_copy(mb[:, :], stS[:, :]),
                            waits=[(mm, t7)], incs=[(ev, 1)])
                t9 = S.emit("vector", lambda E: E.tensor_copy(rb[:, :], stQ[:, :]),
                            waits=[(ev, t8)], incs=[(ev, 1)])
                toks["mbrb"] = t9
                S.emit("scalar", lambda E: E.nop(),
                       waits=[(s_yst[0], s_yst[0].n), (s_yst[1], s_yst[1].n)])
                barrier(S)
                S.run(nc)

    with nc.sbuf_tensor("x2", [128, 32 * 1024], F32) as x2:
        with (
            nc.sbuf_tensor("up0", [128, 2048], BF16) as up0,
            nc.sbuf_tensor("up1", [128, 2048], BF16) as up1,
            nc.sbuf_tensor("dn0", [128, 2048], BF16) as dn0,
            nc.sbuf_tensor("dn1", [128, 2048], BF16) as dn1,
            nc.sbuf_tensor("xp0", [128, 2048], BF16) as xp0,
            nc.sbuf_tensor("g16a", [128, 8 * 1024], BF16) as g16a,
            nc.sbuf_tensor("g16b", [128, 8 * 1024], BF16) as g16b,
            nc.sbuf_tensor("yr0", [128, 1024], F32) as yr0,
            nc.sbuf_tensor("yr1", [128, 1024], F32) as yr1,
            nc.sbuf_tensor("z16s", [128, 1024], BF16) as z16s,
            nc.sbuf_tensor("zsqs", [128, 1024], BF16) as zsqs,
            nc.sbuf_tensor("rowb", [128, 1024], F32) as rowb,
            nc.psum_tensor("upA", [128, 1024], F32) as upA,
            nc.psum_tensor("upB", [128, 1024], F32) as upB,
            nc.psum_tensor("dnA", [128, 1024], F32) as dnA,
            nc.psum_tensor("stP", [128, 1024], F32) as stP,
        ):
            upb = [up0, up1]
            dnb = [dn0, dn1]
            xpb = [xp0, xp0]
            g16 = [g16a, g16b]
            yr = [yr0, yr1]
            up_ps = [upA, upB]
            S = Sched()
            x2_tok = [None] * 32
            yr_free = {0: None, 1: None}
            norm_state = {"next": 0}

            def emit_norm(c):
                p = c % 2
                eng, msem = (("gpsimd", gx) if c % 3 == 2 else ("vector", ev))
                t_l = S.emit("sync", lambda E, p=p, c=c: E.dma_start(yr[p][:, :], ysc.ap()[c]),
                             waits=[yr_free[p]] if yr_free[p] else [], incs=[(s_yr[p], 16)])
                t_s = S.emit(eng,
                             lambda E, p=p: E.tensor_tensor(yr[p][:, :], yr[p][:, :], mb[:, :],
                                                            op=ALU.subtract),
                             waits=[(s_yr[p], t_l), (ev, toks["mbrb"]),
                                    (x2_tok[c - 2] if c >= 2 else None) or (ev, None)],
                             incs=[(msem, 1)])
                x2_tok[c] = (msem, S.emit(eng,
                                          lambda E, p=p, c=c: E.tensor_tensor(
                                              x2[:, c * 1024:(c + 1) * 1024], yr[p][:, :], rb[:, :],
                                              op=ALU.mult),
                                          waits=[(msem, t_s)], incs=[(msem, 1)]))
                yr_free[p] = x2_tok[c]

            def ensure_norm(upto):
                while norm_state["next"] <= min(upto, 31):
                    emit_norm(norm_state["next"])
                    norm_state["next"] += 1

            res_tok = [None] * 32
            upfree = {0: None, 1: None}
            dnfree = {0: None, 1: None}
            xp_war = {0: None}
            upps_war = {0: None, 1: None}
            g16_war = {0: None, 1: None}
            gelu_last = {0: None, 1: None}
            dn_tok = {0: None, 1: None}
            stat_war = None
            stat_last = None
            pend_stats = []

            def emit_stats():
                (c, t_sqz, t_z) = pend_stats.pop(0)

                def fnStats(E, c=c):
                    E.matmul(stP[32:33, 0:512], ones16[:, :], zsqs[:, 0:512],
                             start=(c == 0), stop=(c == 31), skip_group_check=True)
                    E.matmul(stP[32:33, 512:1024], ones16[:, :], zsqs[:, 512:1024],
                             start=(c == 0), stop=(c == 31), skip_group_check=True)
                    E.matmul(stP[0:1, 0:512], ones16[:, :], z16s[:, 0:512],
                             start=False, stop=(c == 31), skip_group_check=True)
                    return E.matmul(stP[0:1, 512:1024], ones16[:, :], z16s[:, 512:1024],
                                    start=False, stop=(c == 31), skip_group_check=True)
                return S.emit("tensor", fnStats,
                              waits=[(ex, t_sqz), (ev, t_z)], incs=[(mm, 1)])

            def queue_stats(c):
                t_sqz = S.emit("scalar",
                               lambda E, c=c: E.activation(zsqs[:, :],
                                                           x2[:, c * 1024:(c + 1) * 1024],
                                                           AF.Square),
                               waits=[(ev, res_tok[c]), (mm, stat_war)], incs=[(ex, 1)])
                t_z = S.emit("vector",
                             lambda E, c=c: E.tensor_copy(z16s[:, :],
                                                          x2[:, c * 1024:(c + 1) * 1024]),
                             waits=[(ev, res_tok[c]), (mm, stat_war)], incs=[(ev, 1)])
                pend_stats.append((c, t_sqz, t_z))

            def emit_dn(h, dc):
                hp = h % 2
                cidx = 2 * h + dc

                def fnDn(E, hp=hp, dc=dc):
                    ins = None
                    for t in range(8):
                        for nb in range(2):
                            ins = E.matmul(dnA[:, nb * 512:(nb + 1) * 512],
                                           dnb[hp][:, t * 256 + dc * 128: t * 256 + dc * 128 + 128],
                                           g16[hp][:, t * 1024 + nb * 512: t * 1024 + nb * 512 + 512],
                                           start=(t == 0), stop=(t == 7), skip_group_check=True)
                    return ins
                waits = [(ex, gelu_last[hp])]
                if cidx >= 1:
                    waits.append((ev, res_tok[cidx - 1]))
                if dc == 0:
                    waits.append((s_dn[hp], dn_tok[hp]))
                t_dmm = S.emit("tensor", fnDn, waits=waits, incs=[(mm, 1)])
                res_tok[cidx] = S.emit(
                    "vector",
                    lambda E, cidx=cidx: E.tensor_tensor(
                        x2[:, cidx * 1024:(cidx + 1) * 1024], dnA[:, :],
                        x2[:, cidx * 1024:(cidx + 1) * 1024], op=ALU.add),
                    waits=[(mm, t_dmm)], incs=[(ev, 1)])
                queue_stats(cidx)
                if dc == 1:
                    dnfree[hp] = t_dmm
                    g16_war[hp] = t_dmm
                return t_dmm

            for h in range(16):
                hp = h % 2
                ensure_norm(2 * h + 3)
                t_up = S.emit("gpsimd", lambda E, hp=hp, h=h: E.dma_start(upb[hp][:, :], up_h.ap()[h]),
                              waits=[(mm, upfree[hp])], incs=[(s_up[hp], 16)])
                dn_tok[hp] = S.emit("gpsimd", lambda E, hp=hp, h=h: E.dma_start(dnb[hp][:, :], dn_h.ap()[h]),
                                    waits=[(mm, dnfree[hp])], incs=[(s_dn[hp], 16)])
                t_xp = S.emit("vector",
                              lambda E, h=h: E.tensor_copy(xpb[0][:, :],
                                                           x2[:, 2 * h * 1024:(2 * h + 2) * 1024]),
                              waits=[x2_tok[2 * h], x2_tok[2 * h + 1], (mm, xp_war[0])],
                              incs=[(ev, 1)])
                up_last = None
                for mc in range(8):
                    mp = mc % 2

                    def fnUp(E, hp=hp, mc=mc, mp=mp):
                        ins = None
                        for dt in range(2):
                            for nb in range(2):
                                ins = E.matmul(up_ps[mp][:, nb * 512:(nb + 1) * 512],
                                               upb[hp][:, dt * 1024 + mc * 128: dt * 1024 + (mc + 1) * 128],
                                               xpb[0][:, dt * 1024 + nb * 512: dt * 1024 + nb * 512 + 512],
                                               start=(dt == 0), stop=(dt == 1), skip_group_check=True)
                        return ins
                    waits = [(ev, t_xp), (ex, upps_war[mp])]
                    if mc == 0:
                        waits.append((s_up[hp], t_up))
                    t_umm = S.emit("tensor", fnUp, waits=waits, incs=[(mm, 1)])
                    up_last = t_umm
                    gwaits = [(mm, t_umm), (mm, g16_war[hp])]
                    gelu_last[hp] = S.emit("scalar",
                                           lambda E, hp=hp, mp=mp, mc=mc: E.activation(
                                               g16[hp][:, mc * 1024:(mc + 1) * 1024],
                                               up_ps[mp][:, :], AF.Gelu),
                                           waits=gwaits, incs=[(ex, 1)])
                    upps_war[mp] = gelu_last[hp]
                    if h > 0 and mc == 2:
                        emit_dn(h - 1, 0)
                    if h > 0 and mc == 5:
                        emit_dn(h - 1, 1)
                    if pend_stats and mc in (3, 6):
                        stat_war = emit_stats()
                        stat_last = stat_war
                upfree[hp] = up_last
                xp_war[0] = up_last
            emit_dn(15, 0)
            stat_war = emit_stats()
            stat_last = stat_war
            emit_dn(15, 1)
            while pend_stats:
                stat_war = emit_stats()
                stat_last = stat_war

            m_r = rowb[0:1, :]
            v_r = yr1[0:1, :]
            l_r = dnA[0:1, 0:1024]
            r_r = yr0[0:1, :]
            t1 = S.emit("vector", lambda E: E.tensor_scalar_mul(m_r, stP[0:1, :], 1.0 / 4096.0),
                        waits=[(mm, stat_last)], incs=[(ev, 1)])
            t2 = S.emit("vector", lambda E: E.tensor_scalar_mul(v_r, stP[32:33, :], 1.0 / 4096.0),
                        waits=[(ev, t1)], incs=[(ev, 1)])
            t3 = S.emit("vector", lambda E: E.tensor_tensor(l_r, m_r, m_r, op=ALU.mult),
                        waits=[(ev, t2)], incs=[(ev, 1)])
            t4 = S.emit("vector", lambda E: E.tensor_tensor(v_r, v_r, l_r, op=ALU.subtract),
                        waits=[(ev, t3)], incs=[(ev, 1)])
            t4b = S.emit("vector", lambda E: E.tensor_scalar_add(v_r, v_r, EPS),
                         waits=[(ev, t4)], incs=[(ev, 1)])
            t5 = S.emit("scalar", lambda E: E.activation(l_r, v_r, AF.Ln),
                        waits=[(ev, t4b)], incs=[(ex, 1)])
            t6 = S.emit("scalar", lambda E: E.activation(r_r, l_r, AF.Exp, scale=-0.5),
                        waits=[(ex, t5)], incs=[(ex, 1)])

            def fn_bc(E):
                E.matmul(stP[:, 0:512], ones32[:, :], m_r[:, 0:512],
                         start=True, stop=True, skip_group_check=True)
                E.matmul(stP[:, 512:1024], ones32[:, :], m_r[:, 512:1024],
                         start=True, stop=True, skip_group_check=True)
                E.matmul(dnA[:, 0:512], ones32[:, :], r_r[:, 0:512],
                         start=True, stop=True, skip_group_check=True)
                return E.matmul(dnA[:, 512:1024], ones32[:, :], r_r[:, 512:1024],
                                start=True, stop=True, skip_group_check=True)
            t7 = S.emit("tensor", fn_bc, waits=[(ex, t6), (ev, t2), (ev, res_tok[31])],
                        incs=[(mm, 1)])
            t8 = S.emit("vector", lambda E: E.tensor_copy(mb[:, :], stP[:, :]),
                        waits=[(mm, t7)], incs=[(ev, 1)])
            t9 = S.emit("vector", lambda E: E.tensor_copy(rb[:, :], dnA[:, :]),
                        waits=[(ev, t8)], incs=[(ev, 1)])
            mbrb2 = t9

            ob_free = {0: None, 1: None}
            tm_free = {0: None, 1: None}
            for c in range(32):
                p = c % 2
                eng, msem = (("gpsimd", gx) if c % 3 == 2 else ("vector", ev))
                t_s = S.emit(eng,
                             lambda E, c=c: E.tensor_tensor(x2[:, c * 1024:(c + 1) * 1024],
                                                            x2[:, c * 1024:(c + 1) * 1024],
                                                            mb[:, :], op=ALU.subtract),
                             waits=[(ev, mbrb2)], incs=[(msem, 1)])
                t_m = S.emit(eng,
                             lambda E, p=p, c=c: E.tensor_tensor(yr[p][:, :],
                                                                 x2[:, c * 1024:(c + 1) * 1024],
                                                                 rb[:, :], op=ALU.mult),
                             waits=[(msem, t_s), (s_ost[p], ob_free[p]),
                                    tm_free[p] or (ev, None)],
                             incs=[(msem, 1)])
                ob_free[p] = S.emit("scalar",
                                    lambda E, p=p, c=c: E.dma_start(out_h.ap()[c], yr[p][:, :]),
                                    waits=[(msem, t_m)], incs=[(s_ost[p], 16)])
                tm_free[p] = (msem, t_m)
            S.emit("scalar", lambda E: E.nop(),
                   waits=[(s_ost[0], s_ost[0].n), (s_ost[1], s_ost[1].n)])
            barrier(S)
            S.run(nc)

    ctx2.close()
    ctx.close()
    return nc


_NC_CACHE = [None]


def _host_stage(inputs):
    X = np.asarray(inputs["L0_attn_hidden_states"], dtype=np.float32)
    Wq = np.asarray(inputs["L0_attn_Wq"], dtype=np.float32)
    Wk = np.asarray(inputs["L0_attn_Wk"], dtype=np.float32)
    Wv = np.asarray(inputs["L0_attn_Wv"], dtype=np.float32)
    bv = np.asarray(inputs["L0_attn_bv"], dtype=np.float32)
    Wo = np.asarray(inputs["L0_attn_Wo"], dtype=np.float32)
    up = np.asarray(inputs["L0_ffn_up"], dtype=np.float32)
    down = np.asarray(inputs["L0_ffn_down"], dtype=np.float32)
    perm = np.asarray(inputs["L0_ffn_perm"]).astype(np.int64)

    XT = [np.ascontiguousarray(X[b].T[perm, :]) for b in range(2)]
    xt_tb = [np.ascontiguousarray(
        XT[b].reshape(32, 128, NTB, TB).transpose(2, 1, 0, 3)).reshape(NTB, 128, 32 * TB)
        for b in range(2)]
    wk_host = np.ascontiguousarray(
        Wk.T[perm].reshape(32, 128, 8, 128).transpose(1, 2, 0, 3)).reshape(128, 8 * 32 * 128)
    wv_host = np.ascontiguousarray(
        Wv.T[perm].reshape(32, 128, 1024).transpose(1, 0, 2)).reshape(128, 32 * 1024)
    wq_host = np.ascontiguousarray(
        Wq.T[perm].reshape(32, 128, 32, 128).transpose(2, 1, 0, 3)).reshape(32, 128, 32 * 128)
    wo_host = np.ascontiguousarray(
        Wo[perm].T.reshape(32, 128, 32, 128).transpose(2, 1, 0, 3)).reshape(32, 128, 32 * 128)
    bv_host = np.ascontiguousarray(bv.reshape(1, 1024))
    up_host = np.ascontiguousarray(
        up.reshape(16, 2, 128, 1024).transpose(0, 2, 1, 3)).reshape(16, 128, 2048)
    dn_host = np.ascontiguousarray(
        down.reshape(16, 8, 128, 256).transpose(0, 2, 1, 3)).reshape(16, 128, 2048)

    in_maps = []
    for core in range(8):
        b, t0 = core // 4, (core % 4) * 1024
        own = XT[b][:, t0:t0 + 1024]
        xo_b = np.ascontiguousarray(
            own.reshape(32, 128, 1024).transpose(1, 0, 2)).reshape(128, 32 * 1024)
        xo_res = np.ascontiguousarray(own.reshape(32, 128, 1024))
        in_maps.append({
            "xt_tb": xt_tb[b], "xo_b": xo_b, "xo_res": xo_res,
            "wk": wk_host, "wv": wv_host, "wq": wq_host, "wo": wo_host,
            "bv": bv_host, "up": up_host, "dn": dn_host,
        })
    return in_maps, perm


def kernel(**inputs):
    in_maps, perm = _host_stage(inputs)
    if _NC_CACHE[0] is None:
        _NC_CACHE[0] = build()
    nc = _NC_CACHE[0]
    res = run_bass_kernel_spmd(nc, in_maps, core_ids=list(range(8)))
    out = np.empty((2, 4096, 4096), dtype=np.float32)
    for core in range(8):
        b, t0 = core // 4, (core % 4) * 1024
        o = np.asarray(res.results[core]["out"]).reshape(4096, 1024)
        out[b, t0:t0 + 1024][:, perm] = o.T
    return out


# version 29 (final)
# speedup vs baseline: 1.0406x; 1.0121x over previous
import numpy as np
from contextlib import ExitStack

import concourse.bass as bass
import concourse.mybir as mybir
from concourse.bass_utils import run_bass_kernel_spmd

F32 = mybir.dt.float32
BF16 = mybir.dt.bfloat16
AF = mybir.ActivationFunctionType
ALU = mybir.AluOpType

SCALE = float(2.0 ** -3.5)
EPS = 1e-5
TB = 256
NTB = 4096 // TB


class Sem:
    def __init__(self, handle):
        self.h = handle
        self.n = 0


class Sched:
    ENGINES = ("tensor", "vector", "scalar", "gpsimd", "sync")

    def __init__(self):
        self.ops = {e: [] for e in self.ENGINES}

    def emit(self, eng, fn, waits=(), incs=()):
        best = {}
        for (s, v) in waits:
            if v:
                best[s.h] = max(best.get(s.h, 0), v)
        w = list(best.items())
        ii = []
        tok = None
        for (s, k) in incs:
            s.n += k
            ii.append((s.h, k))
            tok = s.n
        self.ops[eng].append((w, fn, ii))
        return tok

    def run(self, nc):
        with nc.Block() as blk:
            for name in self.ENGINES:
                ops = self.ops[name]
                if not ops:
                    continue

                def mk(ops=ops):
                    def f(eng):
                        for waits, fn, incs in ops:
                            for (sem, v) in waits:
                                eng.wait_ge(sem, v)
                            ins = fn(eng)
                            for (sem, k) in incs:
                                ins = ins.then_inc(sem, k)
                    return f
                getattr(blk, name)(mk())


def build():
    nc = bass.Bass("TRN2", target_bir_lowering=False)

    xt_tb = nc.dram_tensor("xt_tb", [NTB, 128, 32 * TB], F32, kind="ExternalInput")
    xo_b = nc.dram_tensor("xo_b", [128, 32 * 1024], F32, kind="ExternalInput")
    xo_res = nc.dram_tensor("xo_res", [32, 128, 1024], F32, kind="ExternalInput")
    wk_h = nc.dram_tensor("wk", [128, 8 * 32 * 128], F32, kind="ExternalInput")
    wv_h = nc.dram_tensor("wv", [128, 32 * 1024], F32, kind="ExternalInput")
    wq_h = nc.dram_tensor("wq", [32, 128, 32 * 128], F32, kind="ExternalInput")
    wo_h = nc.dram_tensor("wo", [32, 128, 32 * 128], F32, kind="ExternalInput")
    bv_h = nc.dram_tensor("bv", [1, 1024], F32, kind="ExternalInput")
    up_h = nc.dram_tensor("up", [16, 128, 2048], F32, kind="ExternalInput")
    dn_h = nc.dram_tensor("dn", [16, 128, 2048], F32, kind="ExternalInput")
    out_h = nc.dram_tensor("out", [32, 128, 1024], F32, kind="ExternalOutput")
    ksc = nc.dram_tensor("ksc", [8, 128, 4096], BF16, kind="Internal")
    vsc = nc.dram_tensor("vsc", [32, 128, 8, 128], BF16, kind="Internal")
    qsc = nc.dram_tensor("qsc", [32, 128, 1024], BF16, kind="Internal")
    ysc = nc.dram_tensor("ysc", [32, 128, 1024], F32, kind="Internal")

    ctx = ExitStack()
    ones16 = ctx.enter_context(nc.sbuf_tensor("ones16", [128, 1], BF16))
    ones32 = ctx.enter_context(nc.sbuf_tensor("ones32", [1, 128], F32))

    def sem(name):
        return Sem(ctx.enter_context(nc.semaphore(name)))

    def sem2(name):
        return [sem(name + "0"), sem(name + "1")]

    mm = sem("mm")
    ev = sem("ev")
    ex = sem("ex")
    gx = sem("gx")
    s_wkc = [sem(f"swkc{i}") for i in range(8)]
    s_xoh = [sem("sxoh0"), sem("sxoh1")]
    pv = sem("pv")
    bar = sem("bar")
    s_wk, s_wv, s_bv, s_xob = sem("swk"), sem("swv"), sem("sbv"), sem("sxob")
    s_xb = sem2("sxb")
    s_kst = sem2("skst")
    s_vst = sem2("svst")
    s_wq = sem2("swq")
    s_qst = sem2("sqst")
    s_kb = sem2("skb")
    s_vb = sem2("svb")
    s_qb = sem2("sqb")
    s_wo = sem2("swo")
    s_xr = sem2("sxr")
    s_yst = sem2("syst")
    s_yr = sem2("syr")
    s_up = sem2("sup")
    s_dn = sem2("sdn")
    s_ost = sem2("sost")

    def barrier(S):
        for e in Sched.ENGINES:
            S.emit(e, lambda E: E.nop(), incs=[(bar, 1)])
        tgt = bar.n
        for e in Sched.ENGINES:
            S.emit(e, lambda E: E.nop(), waits=[(bar, tgt)])

    toks = {}

    with (
        nc.sbuf_tensor("bvrow", [1, 1024], F32) as bvrow,
        nc.sbuf_tensor("bvb", [128, 1024], F32) as bvb,
        nc.sbuf_tensor("wkbuf", [128, 8 * 32 * 128], BF16) as wkbuf,
        nc.sbuf_tensor("wvbuf", [128, 32 * 1024], BF16) as wvbuf,
        nc.sbuf_tensor("xb0", [128, 32 * TB], BF16) as xb0,
        nc.sbuf_tensor("xb1", [128, 32 * TB], BF16) as xb1,
        nc.sbuf_tensor("kstg0", [128, TB], BF16) as kstg0,
        nc.sbuf_tensor("kstg1", [128, TB], BF16) as kstg1,
        nc.sbuf_tensor("vstg0", [128, 1024], BF16) as vstg0,
        nc.sbuf_tensor("vstg1", [128, 1024], BF16) as vstg1,
        nc.psum_tensor("kpsA", [128, 512], F32) as kpsA,
        nc.psum_tensor("kpsB", [128, 512], F32) as kpsB,
        nc.psum_tensor("vpsA", [128, 1024], F32) as vpsA,
        nc.psum_tensor("vpsB", [128, 1024], F32) as vpsB,
        nc.psum_tensor("bvps", [128, 1024], F32) as bvps,
    ):
        xb = [xb0, xb1]
        kstg = [kstg0, kstg1]
        vstg = [vstg0, vstg1]
        kps = [kpsA, kpsB]
        vps = [vpsA, vpsB]
        S = Sched()

        t_bv = S.emit("sync", lambda E: E.dma_start(bvrow[:, :], bv_h.ap()), incs=[(s_bv, 16)])
        t_ones = S.emit("vector", lambda E: E.memset(ones16[:, :], 1.0), incs=[(ev, 1)])
        t_ones2 = S.emit("vector", lambda E: E.memset(ones32[:, :], 1.0), incs=[(ev, 1)])

        def fn_bvbc(E):
            E.matmul(bvps[:, 0:512], ones32[:, :], bvrow[0:1, 0:512],
                     start=True, stop=True, skip_group_check=True)
            return E.matmul(bvps[:, 512:1024], ones32[:, :], bvrow[0:1, 512:1024],
                            start=True, stop=True, skip_group_check=True)
        t_bcmm = S.emit("tensor", fn_bvbc, waits=[(s_bv, t_bv), (ev, t_ones2)], incs=[(mm, 1)])
        t_bvb = S.emit("vector", lambda E: E.tensor_copy(bvb[:, :], bvps[:, :]),
                       waits=[(mm, t_bcmm)], incs=[(ev, 1)])

        xb_tok = [None] * NTB
        wk_tok = [None] * 8
        wk_tok[0] = S.emit("gpsimd",
                           lambda E: E.dma_start(wkbuf[:, 0:4096], wk_h.ap()[:, 0:4096]),
                           incs=[(s_wkc[0], 16)])
        for _tb in range(2):
            xb_tok[_tb] = S.emit("gpsimd",
                                 lambda E, d=xb[_tb], _tb=_tb: E.dma_start(d[:, :], xt_tb.ap()[_tb]),
                                 incs=[(s_xb[_tb], 16)])
        for _c in range(1, 8):
            wk_tok[_c] = S.emit("gpsimd",
                                lambda E, _c=_c: E.dma_start(
                                    wkbuf[:, _c * 4096:(_c + 1) * 4096],
                                    wk_h.ap()[:, _c * 4096:(_c + 1) * 4096]),
                                incs=[(s_wkc[_c], 16)])
        t_wv = S.emit("gpsimd", lambda E: E.dma_start(wvbuf[:, :], wv_h.ap()), incs=[(s_wv, 16)])

        xfree = [None] * NTB
        kfree = {0: None, 1: None}
        vfree = {0: None, 1: None}
        kstg_free = {0: None, 1: None}
        vstg_free = {0: None, 1: None}
        kgrp = 0
        vgrp = 0
        for tb in range(NTB):
            p = tb % 2
            if tb >= 2:
                xb_tok[tb] = S.emit(
                    "gpsimd",
                    lambda E, d=xb[p], tb=tb: E.dma_start(d[:, :], xt_tb.ap()[tb]),
                    waits=[(mm, xfree[tb - 2])],
                    incs=[(s_xb[p], 16)])
            for c in range(8):
                kp = kgrp % 2
                kgrp += 1

                def fnK(E, kp=kp, c=c, p=p):
                    ins = None
                    for t in range(32):
                        ins = E.matmul(kps[kp][:, 0:TB],
                                       wkbuf[:, (c * 32 + t) * 128:(c * 32 + t + 1) * 128],
                                       xb[p][:, t * TB:(t + 1) * TB],
                                       start=(t == 0), stop=(t == 31))
                    return ins
                tkm = S.emit("tensor", fnK,
                             waits=[(s_xb[p], xb_tok[tb]), (s_wkc[c], wk_tok[c]), (ev, kfree[kp])],
                             incs=[(mm, 1)])
                tke = S.emit("vector",
                             lambda E, kp=kp: E.tensor_copy(kstg[kp][:, :], kps[kp][:, 0:TB]),
                             waits=[(mm, tkm), (s_kst[kp], kstg_free[kp])], incs=[(ev, 1)])
                kfree[kp] = tke
                kstg_free[kp] = S.emit(
                    "scalar",
                    lambda E, kp=kp, c=c, tb=tb: E.dma_start(
                        ksc.ap()[c][:, tb * TB:(tb + 1) * TB], kstg[kp][:, :]),
                    waits=[(ev, tke)], incs=[(s_kst[kp], 16)])
            for tc in range(2):
                vp = vgrp % 2
                vgrp += 1

                def fnV(E, vp=vp, tc=tc, p=p):
                    ins = None
                    for t in range(32):
                        for nb in range(2):
                            ins = E.matmul(vps[vp][:, nb * 512:(nb + 1) * 512],
                                           xb[p][:, t * TB + tc * 128: t * TB + tc * 128 + 128],
                                           wvbuf[:, t * 1024 + nb * 512: t * 1024 + (nb + 1) * 512],
                                           start=(t == 0), stop=(t == 31))
                    return ins
                tvm = S.emit("tensor", fnV,
                             waits=[(s_xb[p], xb_tok[tb]), (s_wv, t_wv), (ev, vfree[vp])],
                             incs=[(mm, 1)])
                if tc == 1:
                    xfree[tb] = tvm
                tve = S.emit("vector",
                             lambda E, vp=vp: E.tensor_tensor(vstg[vp][:, :], vps[vp][:, :],
                                                              bvb[:, :], op=ALU.add),
                             waits=[(mm, tvm), (s_vst[vp], vstg_free[vp]), (ev, t_bvb)],
                             incs=[(ev, 1)])
                vfree[vp] = tve
                vstg_free[vp] = S.emit(
                    "scalar",
                    lambda E, vp=vp, kt=tb * 2 + tc: E.dma_start(
                        vsc.ap()[kt], vstg[vp][:, :].rearrange("p (h f) -> p h f", f=128)),
                    waits=[(ev, tve)], incs=[(s_vst[vp], 16)])
        S.emit("scalar", lambda E: E.nop(),
               waits=[(s_kst[0], s_kst[0].n), (s_kst[1], s_kst[1].n),
                      (s_vst[0], s_vst[0].n), (s_vst[1], s_vst[1].n)])
        barrier(S)
        S.run(nc)

    with (
        nc.sbuf_tensor("xob", [128, 32 * 1024], BF16) as xob,
        nc.sbuf_tensor("wqb0", [128, 32 * 128], BF16) as wqb0,
        nc.sbuf_tensor("wqb1", [128, 32 * 128], BF16) as wqb1,
        nc.sbuf_tensor("qstg0", [128, 1024], BF16) as qstg0,
        nc.sbuf_tensor("qstg1", [128, 1024], BF16) as qstg1,
        nc.psum_tensor("qpsA", [128, 1024], F32) as qpsA,
        nc.psum_tensor("qpsB", [128, 1024], F32) as qpsB,
    ):
        wqb = [wqb0, wqb1]
        qstg = [qstg0, qstg1]
        qps = [qpsA, qpsB]
        S = Sched()
        def load_xoh(i):
            return S.emit("gpsimd",
                          lambda E, i=i: E.dma_start(xob[:, i * 16384:(i + 1) * 16384],
                                                     xo_b.ap()[:, i * 16384:(i + 1) * 16384]),
                          incs=[(s_xoh[i], 16)])
        t_xoh = [None, None]
        t_xoh[0] = load_xoh(0)
        wq_tok = [None] * 32
        wqfree = {0: None, 1: None}
        qps_free = {0: None, 1: None}
        qstg_free = {0: None, 1: None}
        for c in range(32):
            p = c % 2
            wq_tok[c] = S.emit("gpsimd",
                               lambda E, p=p, c=c: E.dma_start(wqb[p][:, :], wq_h.ap()[c]),
                               waits=[(mm, wqfree[p])], incs=[(s_wq[p], 16)])
            if c == 0:
                t_xoh[1] = load_xoh(1)

            def fnQh(E, p=p, h=0):
                ins = None
                for t in range(h * 16, h * 16 + 16):
                    for nb in range(2):
                        ins = E.matmul(qps[p][:, nb * 512:(nb + 1) * 512],
                                       wqb[p][:, t * 128:(t + 1) * 128],
                                       xob[:, t * 1024 + nb * 512: t * 1024 + (nb + 1) * 512],
                                       start=(t == 0), stop=(t == 31))
                return ins
            S.emit("tensor", lambda E, p=p: fnQh(E, p, 0),
                   waits=[(s_xoh[0], t_xoh[0]), (s_wq[p], wq_tok[c]), (ev, qps_free[p])],
                   incs=[(mm, 1)])
            tqm = S.emit("tensor", lambda E, p=p: fnQh(E, p, 1),
                         waits=[(s_xoh[1], t_xoh[1])],
                         incs=[(mm, 1)])
            wqfree[p] = tqm
            tqe = S.emit("vector",
                         lambda E, p=p: E.tensor_copy(qstg[p][:, :], qps[p][:, :]),
                         waits=[(mm, tqm), (s_qst[p], qstg_free[p])], incs=[(ev, 1)])
            qps_free[p] = tqe
            qstg_free[p] = S.emit(
                "scalar",
                lambda E, p=p, c=c: E.dma_start(qsc.ap()[c], qstg[p][:, :]),
                waits=[(ev, tqe)], incs=[(s_qst[p], 16)])
        S.emit("scalar", lambda E: E.nop(),
               waits=[(s_qst[0], s_qst[0].n), (s_qst[1], s_qst[1].n)])
        barrier(S)
        S.run(nc)

    ctx2 = ExitStack()
    mb = ctx2.enter_context(nc.sbuf_tensor("mb", [128, 1024], F32))
    rb = ctx2.enter_context(nc.sbuf_tensor("rb", [128, 1024], F32))

    with (
        nc.sbuf_tensor("ctxb", [128, 32 * 1024], BF16) as ctxb,
        nc.sbuf_tensor("wob0", [128, 32 * 128], BF16) as wob0,
        nc.sbuf_tensor("wob1", [128, 32 * 128], BF16) as wob1,
    ):
        wob = [wob0, wob1]
        with ExitStack() as es2:
            invs_row = es2.enter_context(nc.sbuf_tensor("invs_row", [1, 512], F32))
            ln_row = es2.enter_context(nc.sbuf_tensor("ln_row", [1, 512], F32))
            invs_bc = es2.enter_context(nc.sbuf_tensor("invs_bc", [128, 512], F32))
            kb = [es2.enter_context(nc.sbuf_tensor(f"kb{i}", [128, 4096], BF16)) for i in range(2)]
            vb = [es2.enter_context(nc.sbuf_tensor(f"vb{i}", [128, 4096], BF16)) for i in range(2)]
            qb = [es2.enter_context(nc.sbuf_tensor(f"qb{i}", [128, 1024], BF16)) for i in range(2)]
            pb = [es2.enter_context(nc.sbuf_tensor(f"pb{i}", [128, 1024], BF16)) for i in range(4)]
            padd = [es2.enter_context(nc.sbuf_tensor(f"padd{i}", [128, 512], BF16)) for i in range(4)]
            sc_ps = [es2.enter_context(nc.psum_tensor(f"sc{i}", [128, 1024], F32)) for i in range(2)]
            cx_ps = [es2.enter_context(nc.psum_tensor(f"cx{i}", [128, 512], F32)) for i in range(2)]
            sm_ps = [es2.enter_context(nc.psum_tensor(f"sm{i}", [128, 512], F32)) for i in range(2)]
            S = Sched()
            kv_free = {0: None, 1: None}
            q_free = {0: None, 1: None}
            pb_war = {0: None, 1: None, 2: None, 3: None}
            pb_war_v = {0: None, 1: None, 2: None, 3: None}
            pa_war = {0: None, 1: None, 2: None, 3: None}
            sc_exp = {0: None, 1: None}
            cs_free = {0: None, 1: None}
            state = {"last_bc": None, "tail": None, "last_ie": None}

            def emit_tail():
                (u_tok, upar, qh, sco) = state["tail"]
                state["tail"] = None
                t_ln = S.emit("scalar",
                              lambda E, upar=upar: E.activation(ln_row[0:1, :],
                                                                sm_ps[upar][0:1, :], AF.Ln),
                              waits=[(mm, u_tok), (ex, state["last_ie"])], incs=[(ex, 1)])
                t_ie = S.emit("scalar",
                              lambda E: E.activation(invs_row[0:1, :], ln_row[0:1, :],
                                                     AF.Exp, scale=-1.0),
                              waits=[(ex, t_ln), (mm, state["last_bc"])], incs=[(ex, 1)])
                state["last_ie"] = t_ie
                t_bc = S.emit("tensor",
                              lambda E, upar=upar: E.matmul(sm_ps[upar][:, :], ones32[:, :],
                                                            invs_row[0:1, :], start=True,
                                                            stop=True, skip_group_check=True),
                              waits=[(ex, t_ie)], incs=[(mm, 1)])
                state["last_bc"] = t_bc
                t_cp = S.emit("vector",
                              lambda E, upar=upar: E.tensor_copy(invs_bc[:, :], sm_ps[upar][:, :]),
                              waits=[(mm, t_bc)], incs=[(ev, 1)])
                t_cx = S.emit("vector",
                              lambda E, upar=upar, qh=qh, sco=sco: E.tensor_tensor(
                                  ctxb[:, qh * 1024 + sco: qh * 1024 + sco + 512],
                                  cx_ps[upar][:, :], invs_bc[:, :], op=ALU.mult),
                              waits=[(ev, t_cp)], incs=[(ev, 1)])
                cs_free[upar] = t_cx

            for h in range(8):
                hp = h % 2
                t_k = S.emit("sync", lambda E, hp=hp, h=h: E.dma_start(kb[hp][:, :], ksc.ap()[h]),
                             waits=[(mm, kv_free[hp])], incs=[(s_kb[hp], 16)])
                t_v = S.emit("sync",
                             lambda E, hp=hp, h=h: E.dma_start(
                                 vb[hp][:, :].rearrange("p (t f) -> p t f", f=128),
                                 vsc.ap()[:, :, h, :].transpose([1, 0, 2])),
                             waits=[(mm, kv_free[hp])], incs=[(s_vb[hp], 16)])
                for g in range(4):
                    qh = h * 4 + g
                    qp = qh % 2
                    t_q = S.emit("sync", lambda E, qp=qp, qh=qh: E.dma_start(qb[qp][:, :], qsc.ap()[qh]),
                                 waits=[(mm, q_free[qp])], incs=[(s_qb[qp], 16)])
                    for qc in range(2):
                        upar = (qh * 2 + qc) % 2
                        sco = qc * 512

                        def emit_scores(grp, first):
                            cp = grp % 2

                            def fnS(E, grp=grp, cp=cp, hp=hp, qp=qp, sco=sco):
                                E.matmul(sc_ps[cp][:, 0:512],
                                         kb[hp][:, (2 * grp) * 128:(2 * grp + 1) * 128],
                                         qb[qp][:, sco:sco + 512],
                                         start=True, stop=True, skip_group_check=True)
                                return E.matmul(sc_ps[cp][:, 512:1024],
                                                kb[hp][:, (2 * grp + 1) * 128:(2 * grp + 2) * 128],
                                                qb[qp][:, sco:sco + 512],
                                                start=True, stop=True, skip_group_check=True)
                            waits = [(ex, sc_exp[cp])]
                            if first:
                                waits += [(s_kb[hp], t_k), (s_qb[qp], t_q)]
                            return S.emit("tensor", fnS, waits=waits, incs=[(mm, 1)])

                        def emit_exp(grp, tok_s):
                            sp = grp % 4
                            cp = grp % 2
                            t = S.emit("scalar",
                                       lambda E, sp=sp, cp=cp: E.activation(pb[sp][:, :], sc_ps[cp][:, :],
                                                                            AF.Exp, scale=SCALE),
                                       waits=[(mm, tok_s), (mm, pb_war[sp])]
                                       + ([pb_war_v[sp]] if pb_war_v[sp] else []),
                                       incs=[(ex, 1)])
                            sc_exp[cp] = t
                            return t

                        def emit_padd(grp, tok_e):
                            sp = grp % 4
                            eng, msem = (("vector", pv) if grp % 2 == 0 else ("gpsimd", gx))
                            t = S.emit(eng,
                                       lambda E, sp=sp: E.tensor_tensor(
                                           padd[sp][:, :], pb[sp][:, 0:512],
                                           pb[sp][:, 512:1024], op=ALU.add),
                                       waits=[(ex, tok_e), (mm, pa_war[sp])], incs=[(msem, 1)])
                            pb_war_v[sp] = (msem, t)
                            return (msem, t)

                        def emit_ctx(grp, tok_e):
                            sp = grp % 4

                            def fnC(E, grp=grp, sp=sp, hp=hp, upar=upar):
                                E.matmul(cx_ps[upar][:, :],
                                         vb[hp][:, (2 * grp) * 128:(2 * grp + 1) * 128],
                                         pb[sp][:, 0:512],
                                         start=(grp == 0), stop=False, skip_group_check=True)
                                return E.matmul(cx_ps[upar][:, :],
                                                vb[hp][:, (2 * grp + 1) * 128:(2 * grp + 2) * 128],
                                                pb[sp][:, 512:1024],
                                                start=False, stop=(grp == 15), skip_group_check=True)
                            waits = [(ex, tok_e)]
                            if grp == 0:
                                waits += [(s_vb[hp], t_v), (ev, cs_free[upar])]
                            t = S.emit("tensor", fnC, waits=waits, incs=[(mm, 1)])
                            pb_war[sp] = t
                            return t

                        def emit_sum(grp, tok_p):
                            sp = grp % 4
                            t = S.emit("tensor",
                                       lambda E, sp=sp, grp=grp, upar=upar: E.matmul(
                                           sm_ps[upar][0:1, :], ones16[:, :], padd[sp][:, :],
                                           start=(grp == 0), stop=(grp == 15),
                                           skip_group_check=True),
                                       waits=[tok_p], incs=[(mm, 1)])
                            pa_war[sp] = t
                            return t

                        s_toks = {}
                        s_toks[0] = emit_scores(0, True)
                        s_toks[1] = emit_scores(1, False)
                        p_toks = {}
                        u_tok = None
                        for grp in range(16):
                            tok_e = emit_exp(grp, s_toks[grp])
                            p_toks[grp] = emit_padd(grp, tok_e)
                            sp = grp % 4
                            cp = grp % 2

                            def fnCS(E, grp=grp, sp=sp, cp=cp, hp=hp, qp=qp, sco=sco, upar=upar):
                                E.matmul(cx_ps[upar][:, :],
                                         vb[hp][:, (2 * grp) * 128:(2 * grp + 1) * 128],
                                         pb[sp][:, 0:512],
                                         start=(grp == 0), stop=False, skip_group_check=True)
                                ins = E.matmul(cx_ps[upar][:, :],
                                               vb[hp][:, (2 * grp + 1) * 128:(2 * grp + 2) * 128],
                                               pb[sp][:, 512:1024],
                                               start=False, stop=(grp == 15), skip_group_check=True)
                                if grp + 2 < 16:
                                    g2 = grp + 2
                                    E.matmul(sc_ps[cp][:, 0:512],
                                             kb[hp][:, (2 * g2) * 128:(2 * g2 + 1) * 128],
                                             qb[qp][:, sco:sco + 512],
                                             start=True, stop=True, skip_group_check=True)
                                    ins = E.matmul(sc_ps[cp][:, 512:1024],
                                                   kb[hp][:, (2 * g2 + 1) * 128:(2 * g2 + 2) * 128],
                                                   qb[qp][:, sco:sco + 512],
                                                   start=True, stop=True, skip_group_check=True)
                                return ins
                            waits = [(ex, tok_e)]
                            if grp == 0:
                                waits += [(s_vb[hp], t_v), (ev, cs_free[upar])]
                            t_cs = S.emit("tensor", fnCS, waits=waits, incs=[(mm, 1)])
                            pb_war[sp] = t_cs
                            if grp + 2 < 16:
                                s_toks[grp + 2] = t_cs
                            if grp == 0 and state["tail"] is not None:
                                emit_tail()
                            if grp >= 2:
                                emit_sum(grp - 2, p_toks[grp - 2])
                        emit_sum(14, p_toks[14])
                        u_tok = emit_sum(15, p_toks[15])
                        if qc == 1:
                            q_free[qp] = s_toks[15]
                            if g == 3:
                                kv_free[hp] = u_tok
                        state["tail"] = (u_tok, upar, qh, sco)
            emit_tail()
            for _c in range(2):
                toks[f"wo{_c}"] = S.emit(
                    "gpsimd",
                    lambda E, _c=_c: E.dma_start(wob[_c][:, :], wo_h.ap()[_c]),
                    incs=[(s_wo[_c], 16)])
            barrier(S)
            S.run(nc)

        with (
            nc.psum_tensor("stS", [128, 1024], F32) as stS,
            nc.psum_tensor("stQ", [128, 1024], F32) as stQ,
        ):
            with ExitStack() as es3:
                m_row = es3.enter_context(nc.sbuf_tensor("m_row", [1, 1024], F32))
                msq_row = es3.enter_context(nc.sbuf_tensor("msq_row", [1, 1024], F32))
                m2_row = es3.enter_context(nc.sbuf_tensor("m2_row", [1, 1024], F32))
                var_row = es3.enter_context(nc.sbuf_tensor("var_row", [1, 1024], F32))
                lnv_row = es3.enter_context(nc.sbuf_tensor("lnv_row", [1, 1024], F32))
                rstd_row = es3.enter_context(nc.sbuf_tensor("rstd_row", [1, 1024], F32))
                xr0 = es3.enter_context(nc.sbuf_tensor("xr0", [128, 1024], F32))
                xr1 = es3.enter_context(nc.sbuf_tensor("xr1", [128, 1024], F32))
                yb0 = es3.enter_context(nc.sbuf_tensor("yb0", [128, 1024], F32))
                yb1 = es3.enter_context(nc.sbuf_tensor("yb1", [128, 1024], F32))
                y16a = es3.enter_context(nc.sbuf_tensor("y16a", [128, 1024], BF16))
                y16b = es3.enter_context(nc.sbuf_tensor("y16b", [128, 1024], BF16))
                ysqa = es3.enter_context(nc.sbuf_tensor("ysqa", [128, 1024], BF16))
                ysqb = es3.enter_context(nc.sbuf_tensor("ysqb", [128, 1024], BF16))
                atA = es3.enter_context(nc.psum_tensor("atA", [128, 1024], F32))
                atB = es3.enter_context(nc.psum_tensor("atB", [128, 1024], F32))
                xr = [xr0, xr1]
                yb = [yb0, yb1]
                y16 = [y16a, y16b]
                ysq = [ysqa, ysqb]
                at_ps = [atA, atB]
                S = Sched()
                wofree = {0: None, 1: None}
                xrfree = {0: None, 1: None}
                at_free = {0: None, 1: None}
                yb_free = {0: None, 1: None}
                sq_war = {0: None, 1: None}
                stg_war = {0: None, 1: None}
                t_stats = None
                pend = {"s": None}

                def emit_stats():
                    (p, c, t_sq, t_16) = pend["s"]
                    pend["s"] = None

                    def fnStats(E, p=p, c=c):
                        E.matmul(stS[0:1, 0:512], ones16[:, :], y16[p][:, 0:512],
                                 start=(c == 0), stop=(c == 31), skip_group_check=True)
                        E.matmul(stS[0:1, 512:1024], ones16[:, :], y16[p][:, 512:1024],
                                 start=(c == 0), stop=(c == 31), skip_group_check=True)
                        E.matmul(stQ[0:1, 0:512], ones16[:, :], ysq[p][:, 0:512],
                                 start=(c == 0), stop=(c == 31), skip_group_check=True)
                        return E.matmul(stQ[0:1, 512:1024], ones16[:, :], ysq[p][:, 512:1024],
                                        start=(c == 0), stop=(c == 31), skip_group_check=True)
                    t = S.emit("tensor", fnStats,
                               waits=[(ex, t_sq), (ev, t_16)], incs=[(mm, 1)])
                    stg_war[p] = t
                    return t

                for c in range(32):
                    p = c % 2
                    if c < 2:
                        t_wo = toks[f"wo{c}"]
                    else:
                        t_wo = S.emit("gpsimd", lambda E, p=p, c=c: E.dma_start(wob[p][:, :], wo_h.ap()[c]),
                                      waits=[(mm, wofree[p])], incs=[(s_wo[p], 16)])
                    t_xr = S.emit("sync", lambda E, p=p, c=c: E.dma_start(xr[p][:, :], xo_res.ap()[c]),
                                  waits=[(ev, xrfree[p])], incs=[(s_xr[p], 16)])

                    def fnWo(E, p=p):
                        ins = None
                        for t in range(32):
                            for nb in range(2):
                                ins = E.matmul(at_ps[p][:, nb * 512:(nb + 1) * 512],
                                               wob[p][:, t * 128:(t + 1) * 128],
                                               ctxb[:, t * 1024 + nb * 512: t * 1024 + (nb + 1) * 512],
                                               start=(t == 0), stop=(t == 31), skip_group_check=True)
                        return ins
                    t_at = S.emit("tensor", fnWo,
                                  waits=[(s_wo[p], t_wo), (ev, at_free[p])], incs=[(mm, 1)])
                    wofree[p] = t_at
                    if pend["s"] is not None:
                        t_stats = emit_stats()
                    t_y = S.emit("vector",
                                 lambda E, p=p: E.tensor_tensor(yb[p][:, :], at_ps[p][:, :],
                                                                xr[p][:, :], op=ALU.add),
                                 waits=[(mm, t_at), (s_xr[p], t_xr), (s_yst[p], yb_free[p]),
                                        (ex, sq_war[p])],
                                 incs=[(ev, 1)])
                    at_free[p] = t_y
                    xrfree[p] = t_y
                    yb_free[p] = S.emit("scalar",
                                        lambda E, p=p, c=c: E.dma_start(ysc.ap()[c], yb[p][:, :]),
                                        waits=[(ev, t_y)], incs=[(s_yst[p], 16)])
                    t_sq = S.emit("scalar",
                                  lambda E, p=p: E.activation(ysq[p][:, :], yb[p][:, :], AF.Square),
                                  waits=[(ev, t_y), (mm, stg_war[p])], incs=[(ex, 1)])
                    sq_war[p] = t_sq
                    t_16 = S.emit("vector",
                                  lambda E, p=p: E.tensor_copy(y16[p][:, :], yb[p][:, :]),
                                  waits=[(mm, stg_war[p]), (ev, t_y)], incs=[(ev, 1)])
                    pend["s"] = (p, c, t_sq, t_16)
                t_stats = emit_stats()
                toks["stats1"] = t_stats
                t1 = S.emit("vector", lambda E: E.tensor_scalar_mul(m_row[:, :], stS[0:1, :], 1.0 / 4096.0),
                            waits=[(mm, toks["stats1"])], incs=[(ev, 1)])
                t2 = S.emit("vector", lambda E: E.tensor_scalar_mul(msq_row[:, :], stQ[0:1, :], 1.0 / 4096.0),
                            waits=[(ev, t1)], incs=[(ev, 1)])
                t3 = S.emit("vector", lambda E: E.tensor_tensor(m2_row[:, :], m_row[:, :], m_row[:, :], op=ALU.mult),
                            waits=[(ev, t2)], incs=[(ev, 1)])
                t4 = S.emit("vector", lambda E: E.tensor_tensor(var_row[:, :], msq_row[:, :], m2_row[:, :], op=ALU.subtract),
                            waits=[(ev, t3)], incs=[(ev, 1)])
                t4b = S.emit("vector", lambda E: E.tensor_scalar_add(var_row[:, :], var_row[:, :], EPS),
                             waits=[(ev, t4)], incs=[(ev, 1)])
                t5 = S.emit("scalar", lambda E: E.activation(lnv_row[:, :], var_row[:, :], AF.Ln),
                            waits=[(ev, t4b)], incs=[(ex, 1)])
                t6 = S.emit("scalar", lambda E: E.activation(rstd_row[:, :], lnv_row[:, :], AF.Exp, scale=-0.5),
                            waits=[(ex, t5)], incs=[(ex, 1)])

                def fn_bc(E):
                    E.matmul(stS[:, 0:512], ones32[:, :], m_row[0:1, 0:512],
                             start=True, stop=True, skip_group_check=True)
                    E.matmul(stS[:, 512:1024], ones32[:, :], m_row[0:1, 512:1024],
                             start=True, stop=True, skip_group_check=True)
                    E.matmul(stQ[:, 0:512], ones32[:, :], rstd_row[0:1, 0:512],
                             start=True, stop=True, skip_group_check=True)
                    return E.matmul(stQ[:, 512:1024], ones32[:, :], rstd_row[0:1, 512:1024],
                                    start=True, stop=True, skip_group_check=True)
                t7 = S.emit("tensor", fn_bc, waits=[(ex, t6), (ev, t2)], incs=[(mm, 1)])
                t8 = S.emit("vector", lambda E: E.tensor_copy(mb[:, :], stS[:, :]),
                            waits=[(mm, t7)], incs=[(ev, 1)])
                t9 = S.emit("vector", lambda E: E.tensor_copy(rb[:, :], stQ[:, :]),
                            waits=[(ev, t8)], incs=[(ev, 1)])
                toks["mbrb"] = t9
                S.emit("scalar", lambda E: E.nop(),
                       waits=[(s_yst[0], s_yst[0].n), (s_yst[1], s_yst[1].n)])
                barrier(S)
                S.run(nc)

    with nc.sbuf_tensor("x2", [128, 32 * 1024], F32) as x2:
        with (
            nc.sbuf_tensor("up0", [128, 2048], BF16) as up0,
            nc.sbuf_tensor("up1", [128, 2048], BF16) as up1,
            nc.sbuf_tensor("dn0", [128, 2048], BF16) as dn0,
            nc.sbuf_tensor("dn1", [128, 2048], BF16) as dn1,
            nc.sbuf_tensor("xp0", [128, 2048], BF16) as xp0,
            nc.sbuf_tensor("g16a", [128, 8 * 1024], BF16) as g16a,
            nc.sbuf_tensor("g16b", [128, 8 * 1024], BF16) as g16b,
            nc.sbuf_tensor("yr0", [128, 1024], F32) as yr0,
            nc.sbuf_tensor("yr1", [128, 1024], F32) as yr1,
            nc.sbuf_tensor("tmpA", [128, 1024], F32) as tmpA,
            nc.sbuf_tensor("tmpB", [128, 1024], F32) as tmpB,
            nc.psum_tensor("upA", [128, 1024], F32) as upA,
            nc.psum_tensor("upB", [128, 1024], F32) as upB,
            nc.psum_tensor("dnA", [128, 1024], F32) as dnA,
            nc.psum_tensor("dnB", [128, 1024], F32) as dnB,
        ):
            upb = [up0, up1]
            dnb = [dn0, dn1]
            xpb = [xp0, xp0]
            g16 = [g16a, g16b]
            yr = [yr0, yr1]
            tmp = [tmpA, tmpB]
            up_ps = [upA, upB]
            dn_ps = [dnA, dnB]
            S = Sched()
            x2_tok = [None] * 32
            yr_free = {0: None, 1: None}
            norm_state = {"next": 0}

            def emit_norm(c):
                p = c % 2
                eng, msem = (("gpsimd", gx) if c % 3 == 2 else ("vector", ev))
                t_l = S.emit("sync", lambda E, p=p, c=c: E.dma_start(yr[p][:, :], ysc.ap()[c]),
                             waits=[yr_free[p]] if yr_free[p] else [], incs=[(s_yr[p], 16)])
                t_s = S.emit(eng,
                             lambda E, p=p: E.tensor_tensor(tmp[p][:, :], yr[p][:, :], mb[:, :],
                                                            op=ALU.subtract),
                             waits=[(s_yr[p], t_l), (ev, toks["mbrb"]),
                                    (x2_tok[c - 2] if c >= 2 else None) or (ev, None)],
                             incs=[(msem, 1)])
                x2_tok[c] = (msem, S.emit(eng,
                                          lambda E, p=p, c=c: E.tensor_tensor(
                                              x2[:, c * 1024:(c + 1) * 1024], tmp[p][:, :], rb[:, :],
                                              op=ALU.mult),
                                          waits=[(msem, t_s)], incs=[(msem, 1)]))
                yr_free[p] = (msem, t_s)

            def ensure_norm(upto):
                while norm_state["next"] <= min(upto, 31):
                    emit_norm(norm_state["next"])
                    norm_state["next"] += 1
            res_tok = [None] * 32
            upfree = {0: None, 1: None}
            dnfree = {0: None, 1: None}
            xp_war = {0: None, 1: None}
            upps_war = {0: None, 1: None}
            dnps_free = {0: None, 1: None}
            g16_war = {0: None, 1: None}
            gelu_last = {0: None, 1: None}
            dn_tok = {0: None, 1: None}

            def emit_dn(h, dc):
                hp = h % 2
                dp = dc % 2

                def fnDn(E, hp=hp, dc=dc, dp=dp):
                    ins = None
                    for t in range(8):
                        for nb in range(2):
                            ins = E.matmul(dn_ps[dp][:, nb * 512:(nb + 1) * 512],
                                           dnb[hp][:, t * 256 + dc * 128: t * 256 + dc * 128 + 128],
                                           g16[hp][:, t * 1024 + nb * 512: t * 1024 + nb * 512 + 512],
                                           start=(t == 0), stop=(t == 7), skip_group_check=True)
                    return ins
                waits = [(ex, gelu_last[hp]), (ev, dnps_free[dp])]
                if dc == 0:
                    waits.append((s_dn[hp], dn_tok[hp]))
                t_dmm = S.emit("tensor", fnDn, waits=waits, incs=[(mm, 1)])
                cidx = 2 * h + dc
                res_tok[cidx] = S.emit(
                    "vector",
                    lambda E, dp=dp, cidx=cidx: E.tensor_tensor(
                        x2[:, cidx * 1024:(cidx + 1) * 1024], dn_ps[dp][:, :],
                        x2[:, cidx * 1024:(cidx + 1) * 1024], op=ALU.add),
                    waits=[(mm, t_dmm)], incs=[(ev, 1)])
                dnps_free[dp] = res_tok[cidx]
                if dc == 1:
                    dnfree[hp] = t_dmm
                    g16_war[hp] = t_dmm
                return t_dmm

            for h in range(16):
                hp = h % 2
                ensure_norm(2 * h + 3)
                t_up = S.emit("gpsimd", lambda E, hp=hp, h=h: E.dma_start(upb[hp][:, :], up_h.ap()[h]),
                              waits=[(mm, upfree[hp])], incs=[(s_up[hp], 16)])
                dn_tok[hp] = S.emit("gpsimd", lambda E, hp=hp, h=h: E.dma_start(dnb[hp][:, :], dn_h.ap()[h]),
                                    waits=[(mm, dnfree[hp])], incs=[(s_dn[hp], 16)])
                t_xp = S.emit("vector",
                              lambda E, hp=hp, h=h: E.tensor_copy(xpb[hp][:, :],
                                                                  x2[:, 2 * h * 1024:(2 * h + 2) * 1024]),
                              waits=[x2_tok[2 * h], x2_tok[2 * h + 1], (mm, xp_war[0])],
                              incs=[(ev, 1)])
                up_last = None
                for mc in range(8):
                    mp = mc % 2

                    def fnUp(E, hp=hp, mc=mc, mp=mp):
                        ins = None
                        for dt in range(2):
                            for nb in range(2):
                                ins = E.matmul(up_ps[mp][:, nb * 512:(nb + 1) * 512],
                                               upb[hp][:, dt * 1024 + mc * 128: dt * 1024 + (mc + 1) * 128],
                                               xpb[hp][:, dt * 1024 + nb * 512: dt * 1024 + nb * 512 + 512],
                                               start=(dt == 0), stop=(dt == 1), skip_group_check=True)
                        return ins
                    waits = [(ev, t_xp), (ex, upps_war[mp])]
                    if mc == 0:
                        waits.append((s_up[hp], t_up))
                    t_umm = S.emit("tensor", fnUp, waits=waits, incs=[(mm, 1)])
                    up_last = t_umm
                    gwaits = [(mm, t_umm)]
                    gwaits.append((mm, g16_war[hp]))
                    gelu_last[hp] = S.emit("scalar",
                                           lambda E, hp=hp, mp=mp, mc=mc: E.activation(
                                               g16[hp][:, mc * 1024:(mc + 1) * 1024],
                                               up_ps[mp][:, :], AF.Gelu),
                                           waits=gwaits, incs=[(ex, 1)])
                    upps_war[mp] = gelu_last[hp]
                    if h > 0 and mc == 2:
                        emit_dn(h - 1, 0)
                    if h > 0 and mc == 5:
                        emit_dn(h - 1, 1)
                upfree[hp] = up_last
                xp_war[0] = up_last
            emit_dn(15, 0)
            emit_dn(15, 1)
            barrier(S)
            S.run(nc)

        with (
            nc.psum_tensor("stS2", [128, 1024], F32) as stS2,
            nc.psum_tensor("stQ2", [128, 1024], F32) as stQ2,
        ):
            with ExitStack() as es5:
                m_row = es5.enter_context(nc.sbuf_tensor("m_row5", [1, 1024], F32))
                msq_row = es5.enter_context(nc.sbuf_tensor("msq_row5", [1, 1024], F32))
                m2_row = es5.enter_context(nc.sbuf_tensor("m2_row5", [1, 1024], F32))
                var_row = es5.enter_context(nc.sbuf_tensor("var_row5", [1, 1024], F32))
                lnv_row = es5.enter_context(nc.sbuf_tensor("lnv_row5", [1, 1024], F32))
                rstd_row = es5.enter_context(nc.sbuf_tensor("rstd_row5", [1, 1024], F32))
                z16a = es5.enter_context(nc.sbuf_tensor("z16a", [128, 1024], BF16))
                z16b = es5.enter_context(nc.sbuf_tensor("z16b", [128, 1024], BF16))
                zsqa = es5.enter_context(nc.sbuf_tensor("zsqa", [128, 1024], BF16))
                zsqb = es5.enter_context(nc.sbuf_tensor("zsqb", [128, 1024], BF16))
                tmpC = es5.enter_context(nc.sbuf_tensor("tmpC", [128, 1024], F32))
                tmpD = es5.enter_context(nc.sbuf_tensor("tmpD", [128, 1024], F32))
                ob0 = es5.enter_context(nc.sbuf_tensor("ob0", [128, 1024], F32))
                ob1 = es5.enter_context(nc.sbuf_tensor("ob1", [128, 1024], F32))
                z16 = [z16a, z16b]
                zsq = [zsqa, zsqb]
                tmp5 = [tmpC, tmpD]
                ob = [ob0, ob1]
                S = Sched()
                stg_war = {0: None, 1: None}
                t_stats = None
                for c in range(32):
                    p = c % 2
                    t_sq = S.emit("scalar",
                                  lambda E, p=p, c=c: E.activation(zsq[p][:, :],
                                                                   x2[:, c * 1024:(c + 1) * 1024],
                                                                   AF.Square),
                                  waits=[(ev, res_tok[c]), (mm, stg_war[p])], incs=[(ex, 1)])
                    t_16 = S.emit("vector",
                                  lambda E, p=p, c=c: E.tensor_copy(z16[p][:, :],
                                                                    x2[:, c * 1024:(c + 1) * 1024]),
                                  waits=[(mm, stg_war[p]), (ev, res_tok[c])], incs=[(ev, 1)])

                    def fnStats(E, p=p, c=c):
                        E.matmul(stS2[0:1, 0:512], ones16[:, :], z16[p][:, 0:512],
                                 start=(c == 0), stop=(c == 31), skip_group_check=True)
                        E.matmul(stS2[0:1, 512:1024], ones16[:, :], z16[p][:, 512:1024],
                                 start=(c == 0), stop=(c == 31), skip_group_check=True)
                        E.matmul(stQ2[0:1, 0:512], ones16[:, :], zsq[p][:, 0:512],
                                 start=(c == 0), stop=(c == 31), skip_group_check=True)
                        return E.matmul(stQ2[0:1, 512:1024], ones16[:, :], zsq[p][:, 512:1024],
                                        start=(c == 0), stop=(c == 31), skip_group_check=True)
                    t_stats = S.emit("tensor", fnStats,
                                     waits=[(ex, t_sq), (ev, t_16)], incs=[(mm, 1)])
                    stg_war[p] = t_stats
                toks["stats2"] = t_stats
                t1 = S.emit("vector", lambda E: E.tensor_scalar_mul(m_row[:, :], stS2[0:1, :], 1.0 / 4096.0),
                            waits=[(mm, toks["stats2"])], incs=[(ev, 1)])
                t2 = S.emit("vector", lambda E: E.tensor_scalar_mul(msq_row[:, :], stQ2[0:1, :], 1.0 / 4096.0),
                            waits=[(ev, t1)], incs=[(ev, 1)])
                t3 = S.emit("vector", lambda E: E.tensor_tensor(m2_row[:, :], m_row[:, :], m_row[:, :], op=ALU.mult),
                            waits=[(ev, t2)], incs=[(ev, 1)])
                t4 = S.emit("vector", lambda E: E.tensor_tensor(var_row[:, :], msq_row[:, :], m2_row[:, :], op=ALU.subtract),
                            waits=[(ev, t3)], incs=[(ev, 1)])
                t4b = S.emit("vector", lambda E: E.tensor_scalar_add(var_row[:, :], var_row[:, :], EPS),
                             waits=[(ev, t4)], incs=[(ev, 1)])
                t5 = S.emit("scalar", lambda E: E.activation(lnv_row[:, :], var_row[:, :], AF.Ln),
                            waits=[(ev, t4b)], incs=[(ex, 1)])
                t6 = S.emit("scalar", lambda E: E.activation(rstd_row[:, :], lnv_row[:, :], AF.Exp, scale=-0.5),
                            waits=[(ex, t5)], incs=[(ex, 1)])

                def fn_bc(E):
                    E.matmul(stS2[:, 0:512], ones32[:, :], m_row[0:1, 0:512],
                             start=True, stop=True, skip_group_check=True)
                    E.matmul(stS2[:, 512:1024], ones32[:, :], m_row[0:1, 512:1024],
                             start=True, stop=True, skip_group_check=True)
                    E.matmul(stQ2[:, 0:512], ones32[:, :], rstd_row[0:1, 0:512],
                             start=True, stop=True, skip_group_check=True)
                    return E.matmul(stQ2[:, 512:1024], ones32[:, :], rstd_row[0:1, 512:1024],
                                    start=True, stop=True, skip_group_check=True)
                t7 = S.emit("tensor", fn_bc, waits=[(ex, t6), (ev, t2)], incs=[(mm, 1)])
                t8 = S.emit("vector", lambda E: E.tensor_copy(mb[:, :], stS2[:, :]),
                            waits=[(mm, t7)], incs=[(ev, 1)])
                t9 = S.emit("vector", lambda E: E.tensor_copy(rb[:, :], stQ2[:, :]),
                            waits=[(ev, t8)], incs=[(ev, 1)])
                toks["mbrb2"] = t9
                ob_free = {0: None, 1: None}
                tm_free = {0: None, 1: None}
                for c in range(32):
                    p = c % 2
                    eng, msem = (("gpsimd", gx) if c % 3 == 2 else ("vector", ev))
                    t_s = S.emit(eng,
                                 lambda E, p=p, c=c: E.tensor_tensor(tmp5[p][:, :],
                                                                     x2[:, c * 1024:(c + 1) * 1024],
                                                                     mb[:, :], op=ALU.subtract),
                                 waits=[(ev, toks["mbrb2"]),
                                        (tm_free[p] if tm_free[p] else (ev, None))],
                                 incs=[(msem, 1)])
                    t_m = S.emit(eng,
                                 lambda E, p=p: E.tensor_tensor(ob[p][:, :], tmp5[p][:, :], rb[:, :],
                                                                op=ALU.mult),
                                 waits=[(msem, t_s), (s_ost[p], ob_free[p])], incs=[(msem, 1)])
                    ob_free[p] = S.emit("scalar",
                                        lambda E, p=p, c=c: E.dma_start(out_h.ap()[c], ob[p][:, :]),
                                        waits=[(msem, t_m)], incs=[(s_ost[p], 16)])
                    tm_free[p] = (msem, t_m)
                S.emit("scalar", lambda E: E.nop(),
                       waits=[(s_ost[0], s_ost[0].n), (s_ost[1], s_ost[1].n)])
                barrier(S)
                S.run(nc)

    ctx2.close()
    ctx.close()
    return nc


_NC_CACHE = [None]


def _host_stage(inputs):
    X = np.asarray(inputs["L0_attn_hidden_states"], dtype=np.float32)
    Wq = np.asarray(inputs["L0_attn_Wq"], dtype=np.float32)
    Wk = np.asarray(inputs["L0_attn_Wk"], dtype=np.float32)
    Wv = np.asarray(inputs["L0_attn_Wv"], dtype=np.float32)
    bv = np.asarray(inputs["L0_attn_bv"], dtype=np.float32)
    Wo = np.asarray(inputs["L0_attn_Wo"], dtype=np.float32)
    up = np.asarray(inputs["L0_ffn_up"], dtype=np.float32)
    down = np.asarray(inputs["L0_ffn_down"], dtype=np.float32)
    perm = np.asarray(inputs["L0_ffn_perm"]).astype(np.int64)

    XT = [np.ascontiguousarray(X[b].T[perm, :]) for b in range(2)]
    xt_tb = [np.ascontiguousarray(
        XT[b].reshape(32, 128, NTB, TB).transpose(2, 1, 0, 3)).reshape(NTB, 128, 32 * TB)
        for b in range(2)]
    wk_host = np.ascontiguousarray(
        Wk.T[perm].reshape(32, 128, 8, 128).transpose(1, 2, 0, 3)).reshape(128, 8 * 32 * 128)
    wv_host = np.ascontiguousarray(
        Wv.T[perm].reshape(32, 128, 1024).transpose(1, 0, 2)).reshape(128, 32 * 1024)
    wq_host = np.ascontiguousarray(
        Wq.T[perm].reshape(32, 128, 32, 128).transpose(2, 1, 0, 3)).reshape(32, 128, 32 * 128)
    wo_host = np.ascontiguousarray(
        Wo[perm].T.reshape(32, 128, 32, 128).transpose(2, 1, 0, 3)).reshape(32, 128, 32 * 128)
    bv_host = np.ascontiguousarray(bv.reshape(1, 1024))
    up_host = np.ascontiguousarray(
        up.reshape(16, 2, 128, 1024).transpose(0, 2, 1, 3)).reshape(16, 128, 2048)
    dn_host = np.ascontiguousarray(
        down.reshape(16, 8, 128, 256).transpose(0, 2, 1, 3)).reshape(16, 128, 2048)

    in_maps = []
    for core in range(8):
        b, t0 = core // 4, (core % 4) * 1024
        own = XT[b][:, t0:t0 + 1024]
        xo_b = np.ascontiguousarray(
            own.reshape(32, 128, 1024).transpose(1, 0, 2)).reshape(128, 32 * 1024)
        xo_res = np.ascontiguousarray(own.reshape(32, 128, 1024))
        in_maps.append({
            "xt_tb": xt_tb[b], "xo_b": xo_b, "xo_res": xo_res,
            "wk": wk_host, "wv": wv_host, "wq": wq_host, "wo": wo_host,
            "bv": bv_host, "up": up_host, "dn": dn_host,
        })
    return in_maps, perm


def kernel(**inputs):
    in_maps, perm = _host_stage(inputs)
    if _NC_CACHE[0] is None:
        _NC_CACHE[0] = build()
    nc = _NC_CACHE[0]
    res = run_bass_kernel_spmd(nc, in_maps, core_ids=list(range(8)))
    out = np.empty((2, 4096, 4096), dtype=np.float32)
    for core in range(8):
        b, t0 = core // 4, (core % 4) * 1024
        o = np.asarray(res.results[core]["out"]).reshape(4096, 1024)
        out[b, t0:t0 + 1024][:, perm] = o.T
    return out
